# version 26 (final)
# baseline (speedup 1.0000x reference)
import contextlib
import numpy as np

import concourse.bass as bass
import concourse.mybir as mybir
from concourse.bass_utils import run_bass_kernel_spmd

F32 = mybir.dt.float32
AF = mybir.ActivationFunctionType
ALU = mybir.AluOpType
AX = mybir.AxisListType

T = 2048
D = 1024
H = 16
KH = 64
C = 64
SC = 256
NSC = T // SC
NT = T // 128
NFC = D // 128
GN_EPS = 0.00064
LN_EPS = 1e-5
NORM_EPS2 = 1e-12
FFN_EPS2 = 1e-24
CLAMP_MAX = float(np.log(1.0 / 0.01))

DEBUG = False
PHASES = 3
B_SC = 8
B_STAGE = 5

ENGS = ("tensor", "vector", "scalar", "sync", "gpsimd")
SEMS = ("LD", "LDW0", "LDW1", "LDX0", "LDX1", "LDB",
        "STG0", "STG1", "STG2", "STG3", "PE", "VE", "SE", "GE")
ENG2SEM = {"tensor": "PE", "vector": "VE", "scalar": "SE", "gpsimd": "GE"}


class Emitter:
    def __init__(self):
        self.q = {k: [] for k in ENGS}
        self.cnt = {k: 0 for k in SEMS}
        self.sems = {}

    def wait(self, eng, sem, val):
        if val > 0:
            self.q[eng].append(lambda e, s=sem, v=val: e.wait_ge(self.sems[s], v))

    def op(self, eng, fn, inc=None, n=1):
        if eng in ("vector", "scalar") and self.q[eng]:
            self.q[eng].append(lambda e: e.drain())
        if inc is None:
            self.q[eng].append(fn)
            return None
        self.cnt[inc] += n
        v = self.cnt[inc]
        if eng == "gpsimd":
            self.q[eng].append(fn)
            self.q[eng].append(lambda e: e.drain())
            self.q[eng].append(lambda e, s=inc, k=n: e.nop(nofuse=True).then_inc(self.sems[s], k))
        else:
            self.q[eng].append(lambda e, f=fn, s=inc, k=n: f(e).then_inc(self.sems[s], k))
        return v

    def mark(self, eng):
        sem = ENG2SEM[eng]
        self.cnt[sem] += 1
        v = self.cnt[sem]
        self.q[eng].append(lambda e: e.drain())
        self.q[eng].append(lambda e, s=sem: e.nop(nofuse=True).then_inc(self.sems[s], 1))
        return v

    def barrier(self):
        for eng in ("tensor", "vector", "scalar", "gpsimd"):
            self.mark(eng)
        snap = dict(self.cnt)
        for eng in ENGS:
            for sem in ("PE", "VE", "SE", "GE"):
                if ENG2SEM.get(eng) == sem:
                    continue
                self.wait(eng, sem, snap[sem])
        return snap


def _mm(ps_ap, lhsT, rhs, start, stop):
    def f(e):
        return e.matmul(ps_ap, lhsT, rhs, start=start, stop=stop, skip_group_check=True)
    return f


def _tp(ps_ap, in_ap, ident, start, stop):
    def f(e):
        return e.matmul(ps_ap, in_ap, ident, start=start, stop=stop,
                        is_transpose=True, skip_group_check=True)
    return f


def hbc(ap64, nblk=16):
    return ap64.unsqueeze(1).broadcast_to([128, nblk, 64])


def build(debug=False):
    nc = bass.Bass("TRN2", target_bir_lowering=False)
    em = Emitter()

    x_in = nc.dram_tensor("x_in", [T, D], F32, kind="ExternalInput").ap()
    wT = {m: nc.dram_tensor(f"w_{m}T", [D, D], F32, kind="ExternalInput").ap()
          for m in "rwkvabg"}
    wo_T = nc.dram_tensor("wo_T", [D, D], F32, kind="ExternalInput").ap()
    ffnw_T = nc.dram_tensor("ffnw_T", [D, D], F32, kind="ExternalInput").ap()
    wp_T = nc.dram_tensor("wp_T", [D, 256], F32, kind="ExternalInput").ap()
    wp_nat = nc.dram_tensor("wp_nat", [256, D], F32, kind="ExternalInput").ap()
    wd_T = nc.dram_tensor("wd_T", [D, 15], F32, kind="ExternalInput").ap()
    wu_rho = nc.dram_tensor("wu_rho", [15, D], F32, kind="ExternalInput").ap()
    bo_rep = nc.dram_tensor("bo_rep", [128, D], F32, kind="ExternalInput").ap()
    gnw_rep = nc.dram_tensor("gnw_rep", [128, D], F32, kind="ExternalInput").ap()
    gnb_rep = nc.dram_tensor("gnb_rep", [128, D], F32, kind="ExternalInput").ap()
    bp_rep = nc.dram_tensor("bp_rep", [128, 256], F32, kind="ExternalInput").ap()
    mix_cat = nc.dram_tensor("mix_cat", [128, 56], F32, kind="ExternalInput").ap()
    sim_in = nc.dram_tensor("sim_in", [256, 4], F32, kind="ExternalInput").ap()
    temp_in = nc.dram_tensor("temp_in", [1, 1], F32, kind="ExternalInput").ap()
    out_d = nc.dram_tensor("out", [T, D], F32, kind="ExternalOutput").ap()

    skind = "ExternalOutput" if debug else "Internal"
    scr_fm = {m: nc.dram_tensor(f"scr_{m}", [NFC, 128, T], F32, kind=skind).ap()
              for m in ("r", "p")}
    scr_tm = {m: nc.dram_tensor(f"scr_{m}", [T, D], F32, kind=skind).ap()
              for m in ("k", "v", "a", "b", "g")}
    scr_ng = nc.dram_tensor("scr_ng", [NFC, 128, T], F32, kind=skind).ap()
    dbg_o = nc.dram_tensor("dbg_o", [T, D], F32, kind=skind).ap() if debug else None
    dbg_sk = nc.dram_tensor("dbg_sk", [32, 128, 512], F32, kind=skind).ap() if debug else None
    dbg_kt = nc.dram_tensor("dbg_kt", [NSC, 128, 2048], F32, kind=skind).ap() if debug else None
    dbg_st = nc.dram_tensor("dbg_st", [32, 128, 512], F32, kind=skind).ap() if debug else None
    dbg_ps7 = nc.dram_tensor("dbg_ps7", [32, 128, 512], F32, kind=skind).ap() if debug else None
    dbg_skin = nc.dram_tensor("dbg_skin", [32, 128, 512], F32, kind=skind).ap() if debug else None
    dbg_bt = nc.dram_tensor("dbg_bt", [NSC, 128, 2048], F32, kind=skind).ap() if debug else None
    dbg_at = nc.dram_tensor("dbg_at", [NSC, 128, 2048], F32, kind=skind).ap() if debug else None
    dbg_zt = nc.dram_tensor("dbg_zt", [32, 128, 1024], F32, kind=skind).ap() if debug else None

    with contextlib.ExitStack() as stack:
        e_ = stack.enter_context
        AR = e_(nc.sbuf_tensor("arena", [128, 53120], F32))
        ar = AR.ap()
        PS = [e_(nc.psum_tensor(f"ps{i}", [128, 512], F32)) for i in range(8)]
        ps = [p.ap() for p in PS]
        for s in SEMS:
            em.sems[s] = e_(nc.semaphore(s))
        block = e_(nc.Block())

        off, _sz = {}, {}
        cur = [0]

        def AL(name, n):
            off[name] = cur[0]
            _sz[name] = n
            cur[0] += n
            return off[name]

        def A(name):
            return ar[:, off[name]:off[name] + _sz[name]]

        AL("ident", 128)
        AL("i64", 64)
        AL("maskU", 64)
        AL("maskL", 64)
        AL("maskUI", 64)
        AL("bdU", 128)
        AL("bdL", 128)
        AL("tiny", 16)
        AL("gnw", D)
        AL("gnb", D)
        AL("bo", D)
        AL("bp", 256)
        AL("mix", 56)
        AL("simn", 8)
        AL("simw", 32)
        AL("bpw", 4)
        AL("srep", 4)
        AL("wd", 120)
        AL("wu", 1024)
        AL("sk", 512)
        AL("sc0", 64)
        AL("cap", 8)
        PB = cur[0]
        assert 53120 - PB >= 44000, PB

        pcur = [PB]

        def pal(name, n):
            off[name] = pcur[0]
            _sz[name] = n
            pcur[0] += n
            assert pcur[0] <= 53120, (name, pcur[0])
            return off[name]

        def preset():
            pcur[0] = PB

        IDT = A("ident")
        lane = ar[:, off["tiny"]:off["tiny"] + 1]
        h64 = ar[:, off["tiny"] + 1:off["tiny"] + 2]
        lane64 = ar[:, off["tiny"] + 2:off["tiny"] + 3]

        def load(dst, src, sem="LD"):
            return em.op("sync", lambda e, d=dst, s=src: e.dma_start(out=d, in_=s),
                         inc=sem, n=16)

        def store(dst, src, sem="STG0"):
            return em.op("sync", lambda e, d=dst, s=src: e.dma_start(out=d, in_=s),
                         inc=sem, n=16)

        def wait_all_stores(eng):
            for sg in ("STG0", "STG1", "STG2", "STG3"):
                em.wait(eng, sg, em.cnt[sg])

        fidx128 = ar[:, PB:PB + 128]
        fidx64 = ar[:, off["sc0"]:off["sc0"] + 64]
        em.op("gpsimd", lambda e: e.iota(fidx128, pattern=[[1, 128]], base=0,
                                         channel_multiplier=0,
                                         allow_small_or_imprecise_dtypes=True))
        em.op("gpsimd", lambda e: e.iota(lane, pattern=[[0, 1]], base=0,
                                         channel_multiplier=1,
                                         allow_small_or_imprecise_dtypes=True), inc="GE")
        g0 = em.cnt["GE"]
        em.wait("vector", "GE", g0)
        em.op("vector", lambda e: e.tensor_scalar(A("ident"), fidx128, lane, None, ALU.is_equal))
        em.op("vector", lambda e: e.tensor_scalar(h64, lane, 64.0, None, ALU.is_ge))
        em.op("vector", lambda e: e.scalar_tensor_tensor(lane64, h64, -64.0, lane,
                                                         ALU.mult, ALU.add))
        em.op("vector", lambda e: e.tensor_copy(fidx64, fidx128[:, 0:64]))
        em.op("vector", lambda e: e.tensor_scalar(A("maskU"), fidx64, lane64, None, ALU.is_gt))
        em.op("vector", lambda e: e.tensor_scalar(A("maskL"), fidx64, lane64, None, ALU.is_lt))
        em.op("vector", lambda e: e.tensor_scalar(A("maskUI"), fidx64, lane64, None, ALU.is_ge))
        em.op("vector", lambda e: e.tensor_scalar(A("i64"), fidx64, lane64, None, ALU.is_equal))
        em.op("vector", lambda e: e.memset(A("bdU"), 0.0))
        em.op("vector", lambda e: e.memset(A("bdL"), 0.0))
        em.op("vector", lambda e: e.tensor_copy(A("bdU")[0:64, 0:64], A("maskU")[0:64, :]))
        em.op("vector", lambda e: e.tensor_copy(A("bdU")[64:128, 64:128], A("maskU")[64:128, :]))
        em.op("vector", lambda e: e.tensor_copy(A("bdL")[0:64, 0:64], A("maskL")[0:64, :]))
        em.op("vector", lambda e: e.tensor_copy(A("bdL")[64:128, 64:128], A("maskL")[64:128, :]))
        em.op("vector", lambda e: e.memset(A("sk"), 0.0))
        _cvals = [1.0, LN_EPS, GN_EPS, 1e-30]
        for _i, _v in enumerate(_cvals):
            cap_col = ar[:, off["cap"] + _i:off["cap"] + _i + 1]
            nc.const_aps.register(F32, _v, cap_col)
            em.op("vector", lambda e, c=cap_col, vv=_v: e.memset(c, vv))
        em.op("vector", lambda e: e.nop(), inc="VE")
        v_consts = em.cnt["VE"]

        load(A("gnw"), gnw_rep)
        load(A("gnb"), gnb_rep)
        load(A("bo"), bo_rep)
        load(A("bp"), bp_rep)
        load(A("mix"), mix_cat)
        load(ar[:, off["simn"]:off["simn"] + 4], sim_in[0:128, :])
        load(ar[:, off["simn"] + 4:off["simn"] + 8], sim_in[128:256, :])
        load(ar[0:1, off["srep"] + 2:off["srep"] + 3], temp_in)
        load(A("wd"), wd_T.rearrange("(c p) r -> p c r", p=128))
        load(ar[0:15, off["wu"]:off["wu"] + 1024], wu_rho)
        wpn0 = ar[:, PB + 256:PB + 1280]
        wpn1 = ar[:, PB + 1280:PB + 2304]
        load(wpn0, wp_nat[0:128, :])
        ld0 = load(wpn1, wp_nat[128:256, :])

        sq0 = ar[:, PB + 2304:PB + 2308]
        sq1 = ar[:, PB + 2308:PB + 2312]
        ones_strip = ar[0:1, PB + 2312:PB + 2440]
        em.wait("scalar", "LD", ld0)
        em.wait("scalar", "VE", v_consts)
        em.op("scalar", lambda e: e.activation(sq0, ar[:, off["simn"]:off["simn"] + 4], AF.Square))
        em.op("scalar", lambda e: e.activation(sq1, ar[:, off["simn"] + 4:off["simn"] + 8],
                                               AF.Square), inc="SE")
        s1 = em.cnt["SE"]
        em.wait("vector", "VE", v_consts)
        em.op("vector", lambda e: e.memset(ones_strip, 1.0), inc="VE")
        v1 = em.cnt["VE"]
        onescol = ar[:, PB + 2440:PB + 2441]
        em.op("vector", lambda e: e.memset(onescol, 1.0), inc="VE")
        v1b = em.cnt["VE"]
        em.wait("tensor", "SE", s1)
        em.wait("tensor", "VE", v1b)
        em.op("tensor", _mm(ps[0][0:1, 0:4], onescol, sq0, True, False))
        em.op("tensor", _mm(ps[0][0:1, 0:4], onescol, sq1, False, True), inc="PE")
        p1 = em.cnt["PE"]
        colinv1 = ar[0:1, PB + 2444:PB + 2448]
        em.wait("scalar", "PE", p1)
        em.op("scalar", lambda e: e.activation(colinv1, ps[0][0:1, 0:4], AF.Ln, bias=1e-30))
        em.op("scalar", lambda e: e.activation(colinv1, colinv1, AF.Exp, scale=-0.5), inc="SE")
        s2 = em.cnt["SE"]
        em.wait("tensor", "SE", s2)
        em.op("tensor", _mm(ps[1][:, 0:4], ones_strip, colinv1, True, True), inc="PE")
        p2 = em.cnt["PE"]
        colinv = ar[:, PB + 2448:PB + 2452]
        em.wait("vector", "PE", p2)
        em.op("vector", lambda e: e.tensor_copy(colinv, ps[1][:, 0:4]))
        em.op("vector", lambda e: e.tensor_tensor(ar[:, off["simn"]:off["simn"] + 4],
                                                  ar[:, off["simn"]:off["simn"] + 4],
                                                  colinv, ALU.mult))
        em.op("vector", lambda e: e.tensor_tensor(ar[:, off["simn"] + 4:off["simn"] + 8],
                                                  ar[:, off["simn"] + 4:off["simn"] + 8],
                                                  colinv, ALU.mult))
        em.op("vector", lambda e: e.tensor_scalar(ar[0:1, off["srep"] + 3:off["srep"] + 4],
                                                  ar[0:1, off["srep"] + 2:off["srep"] + 3],
                                                  CLAMP_MAX, None, ALU.min), inc="VE")
        v2 = em.cnt["VE"]
        em.wait("scalar", "VE", v2)
        em.op("scalar", lambda e: e.activation(ar[0:1, off["srep"] + 3:off["srep"] + 4],
                                               ar[0:1, off["srep"] + 3:off["srep"] + 4],
                                               AF.Exp), inc="SE")
        s3 = em.cnt["SE"]
        em.wait("tensor", "SE", s3)
        em.wait("tensor", "VE", v2)
        em.op("tensor", _mm(ps[1][:, 4:5], ones_strip,
                            ar[0:1, off["srep"] + 3:off["srep"] + 4], True, True), inc="PE")
        p3 = em.cnt["PE"]
        em.wait("vector", "PE", p3)
        em.op("vector", lambda e: e.tensor_copy(ar[:, off["srep"]:off["srep"] + 1],
                                                ps[1][:, 4:5]), inc="VE")

        for dc in range(8):
            em.op("tensor", _mm(ps[2][:, 4 * dc:4 * dc + 4], wpn0[:, 128 * dc:128 * dc + 128],
                                ar[:, off["simn"]:off["simn"] + 4], True, False))
            em.op("tensor", _mm(ps[2][:, 4 * dc:4 * dc + 4], wpn1[:, 128 * dc:128 * dc + 128],
                                ar[:, off["simn"] + 4:off["simn"] + 8], False, True),
                  inc="PE" if dc == 7 else None)
        p4 = em.cnt["PE"]
        em.wait("vector", "PE", p4)
        em.op("vector", lambda e: e.tensor_copy(A("simw"), ps[2][:, 0:32]), inc="VE")
        v3 = em.cnt["VE"]
        bpcol = ar[:, PB + 2452:PB + 2454]
        em.op("tensor", _tp(ps[3][:, 0:1], A("bp")[0:1, 0:128], IDT[0:1, 0:1], True, False))
        em.op("tensor", _tp(ps[3][:, 1:2], A("bp")[0:1, 128:256], IDT[0:1, 0:1], False, True), inc="PE")
        p5 = em.cnt["PE"]
        em.wait("vector", "PE", p5)
        em.op("vector", lambda e: e.tensor_copy(bpcol, ps[3][:, 0:2]), inc="VE")
        v4 = em.cnt["VE"]
        em.wait("tensor", "VE", v4)
        em.op("tensor", _mm(ps[3][0:4, 2:3], ar[:, off["simn"]:off["simn"] + 4],
                            bpcol[:, 0:1], True, False))
        em.op("tensor", _mm(ps[3][0:4, 2:3], ar[:, off["simn"] + 4:off["simn"] + 8],
                            bpcol[:, 1:2], False, True), inc="PE")
        p6 = em.cnt["PE"]
        bpw41 = ar[0:4, PB + 2454:PB + 2455]
        em.wait("vector", "PE", p6)
        em.op("vector", lambda e: e.tensor_copy(bpw41, ps[3][0:4, 2:3]), inc="VE")
        v5 = em.cnt["VE"]
        em.wait("tensor", "VE", v5)
        em.op("tensor", _tp(ps[3][0:1, 4:8], bpw41, IDT[0:4, 0:4], True, True), inc="PE")
        p7 = em.cnt["PE"]
        bpwrow = ar[0:1, PB + 2456:PB + 2460]
        em.wait("vector", "PE", p7)
        em.op("vector", lambda e: e.tensor_copy(bpwrow, ps[3][0:1, 4:8]), inc="VE")
        v6 = em.cnt["VE"]
        em.wait("tensor", "VE", v6)
        em.op("tensor", _mm(ps[3][:, 8:12], ones_strip, bpwrow, True, True), inc="PE")
        p8 = em.cnt["PE"]
        em.wait("vector", "PE", p8)
        em.op("vector", lambda e: e.tensor_copy(A("bpw"), ps[3][:, 8:12]), inc="VE")

        em.barrier()

        preset()
        XHW = 2049
        pal("xhf", NFC * XHW)
        pal("wcur", 8192)
        pal("wnxt", 8192)
        pal("stg", 4 * 512)
        pal("xt", 2 * 1024)
        pal("xh_t", 2 * 1024)
        pal("inp", 8 * 512)
        pal("dlt", 2 * 512)
        pal("lnst", 32)
        pal("sqscr", 1024)

        def xhf_ap(fc, c0, clen, lo=0, hi=128):
            base = off["xhf"] + fc * XHW + c0
            return ar[lo:hi, base:base + clen]

        xhf_all = ar[:, off["xhf"]:off["xhf"] + NFC * XHW]
        em.op("vector", lambda e: e.memset(
            xhf_all.rearrange("p (c w) -> p c w", w=XHW)[:, :, 0:1], 0.0), inc="VE")
        v_zcol = em.cnt["VE"]

        se_apply = {}
        ve_evac = {}
        pe_tpL = {}

        def ln1_tp(tt):
            slot = tt % 2
            xh_ap = ar[:, off["xh_t"] + 1024 * slot: off["xh_t"] + 1024 * slot + 1024]
            b0 = 4 + slot * 2
            em.wait("tensor", "SE", se_apply[tt])
            if tt >= 2:
                em.wait("tensor", "VE", ve_evac[tt - 2])
            for fc in range(NFC):
                bank = b0 + fc // 4
                q = fc % 4
                em.op("tensor", _tp(ps[bank][:, 128 * q:128 * q + 128],
                                    xh_ap[:, 128 * fc:128 * fc + 128], IDT,
                                    start=(q == 0), stop=(q == 3)),
                      inc="PE" if fc == NFC - 1 else None)
            pe_tp = em.cnt["PE"]
            pe_tpL[tt] = pe_tp
            em.wait("vector", "PE", pe_tp)
            for half in range(2):
                bank = b0 + half
                dst = xhf_all.rearrange("p (c w) -> p c w", w=XHW)[
                    :, 4 * half:4 * half + 4, 1 + 128 * tt:1 + 128 * tt + 128]
                em.op("vector", lambda e, b=bank, d=dst: e.tensor_copy(
                    d, ps[b][:, 0:512].rearrange("p (c w) -> p c w", w=128)),
                    inc="VE" if half == 1 else None)
            ve_evac[tt] = em.cnt["VE"]

        for tt in range(NT):
            slot = tt % 2
            xt_ap = ar[:, off["xt"] + 1024 * slot: off["xt"] + 1024 * slot + 1024]
            xh_ap = ar[:, off["xh_t"] + 1024 * slot: off["xh_t"] + 1024 * slot + 1024]
            if tt >= 2:
                em.wait("sync", "SE", se_apply[tt - 2])
                em.wait("sync", "PE", pe_tpL[tt - 2])
            ldx = load(xt_ap, x_in[128 * tt:128 * tt + 128, :], sem=f"LDX{slot}")
            em.wait("vector", f"LDX{slot}", ldx)
            em.wait("scalar", f"LDX{slot}", ldx)
            stv = ar[:, off["lnst"]:off["lnst"] + 8]
            red_s, qsum, mu, m2, var, rinv, beta = (stv[:, i:i + 1] for i in range(7))
            sqscr = ar[:, off["sqscr"]:off["sqscr"] + 1024]
            em.op("vector", lambda e, a=xt_ap, r=red_s: e.tensor_reduce(r, a, AX.X, ALU.add))
            em.op("scalar", lambda e, a=xt_ap, r=qsum, s=sqscr: e.activation(
                s, a, AF.Square, accum_out=r), inc="SE")
            s_q = em.cnt["SE"]
            em.op("vector", lambda e, r=red_s, m=mu: e.tensor_scalar(m, r, 1.0 / D, None,
                                                                     ALU.mult))
            em.op("vector", lambda e, m=mu, o=m2: e.tensor_tensor(o, m, m, ALU.mult))
            em.wait("vector", "SE", s_q)
            em.op("vector", lambda e, q=qsum, o=var, m=m2: e.scalar_tensor_tensor(
                o, q, 1.0 / D, m, ALU.mult, ALU.subtract), inc="VE")
            v_var = em.cnt["VE"]
            em.wait("scalar", "VE", v_var)
            em.op("scalar", lambda e, v_=var, r=rinv: e.activation(r, v_, AF.Ln, bias=LN_EPS))
            em.op("scalar", lambda e, r=rinv: e.activation(r, r, AF.Exp, scale=-0.5), inc="SE")
            s_rinv = em.cnt["SE"]
            em.wait("vector", "SE", s_rinv)
            em.op("vector", lambda e, m=mu, r=rinv, b=beta: e.tensor_scalar(
                b, m, r, -1.0, ALU.mult, ALU.mult), inc="VE")
            v_beta = em.cnt["VE"]
            em.wait("scalar", "VE", v_beta)
            se_apply[tt] = em.op("scalar", lambda e, xs=xt_ap, xo=xh_ap, r=rinv, b=beta:
                                 e.activation(xo, xs, AF.Identity, scale=r, bias=b), inc="SE")
            if tt >= 1:
                ln1_tp(tt - 1)
        ln1_tp(NT - 1)

        weights = ["r", "w", "k", "v", "a", "b", "g"]
        wslot_off = [off["wcur"], off["wnxt"]]
        ld_w = {}
        pe_wdone = {}

        def load_weight(mi):
            slot = mi % 2
            if mi >= 2:
                em.wait("sync", "PE", pe_wdone[mi - 2])
            v = None
            for kc in range(8):
                dst = ar[:, wslot_off[slot] + 1024 * kc: wslot_off[slot] + 1024 * kc + 1024]
                v = load(dst, wT[weights[mi]][128 * kc:128 * kc + 128, :], sem=f"LDW{slot}")
            ld_w[mi] = (f"LDW{slot}", v)

        load_weight(0)
        load_weight(1)

        stg_last = [None, None, None, None]
        stg_idx = [0]

        def stg_slot():
            i = stg_idx[0] % 4
            stg_idx[0] += 1
            return i, ar[:, off["stg"] + 512 * i: off["stg"] + 512 * i + 512]

        bank_guard = [("VE", 0)] * 8
        inp_prev_guard = [0] * 16
        for _j in range(8, 16):
            inp_prev_guard[_j] = pe_tpL[NT - 1]
        em.wait("vector", "SE", se_apply[NT - 1])
        em.wait("vector", "VE", v_zcol)
        em.wait("gpsimd", "VE", em.cnt["VE"])

        def inp_off(sl2):
            if sl2 < 8:
                return off["inp"] + 512 * sl2
            return off["xt"] + 512 * (sl2 - 8)

        jobs = [(mi2, t42) for mi2 in range(len(weights)) for t42 in range(4)]
        v_inp_j = {}

        def build_inp(j):
            mi2, t42 = jobs[j]
            t0b = 512 * t42
            sb = 8 * (j % 2)
            for kc in range(8):
                inp_ap = ar[:, inp_off(sb + kc): inp_off(sb + kc) + 512]
                dlt_ap = ar[:, off["dlt"]: off["dlt"] + 512]
                guard = inp_prev_guard[sb + kc]
                xh_sl = xhf_ap(kc, 1 + t0b, 512)
                xh_sh = xhf_ap(kc, t0b, 512)
                if guard:
                    em.wait("vector", "PE", guard)
                em.op("vector", lambda e, d2=dlt_ap, a=xh_sh, b=xh_sl: e.tensor_tensor(
                    d2, a, b, ALU.subtract))
                mixcol = ar[:, off["mix"] + 8 * mi2 + kc: off["mix"] + 8 * mi2 + kc + 1]
                em.op("vector", lambda e, i2=inp_ap, d2=dlt_ap, mc=mixcol, x=xh_sl:
                      e.scalar_tensor_tensor(i2, d2, mc, x, ALU.mult, ALU.add), inc="VE")
            v_inp_j[j] = em.cnt["VE"]

        build_inp(0)
        for j, (mi, t4) in enumerate(jobs):
            m = weights[mi]
            if t4 == 0:
                em.wait("tensor", ld_w[mi][0], ld_w[mi][1])
            wbase = wslot_off[mi % 2]
            fm = m in ("r", "w")
            if True:
                t0 = 512 * t4
                sbase = 8 * (j % 2)
                if j + 1 < len(jobs):
                    build_inp(j + 1)
                em.wait("tensor", "VE", v_inp_j[j])

                def inp_sl(kc, c0=0, clen=512):
                    return ar[:, inp_off(sbase + kc) + c0: inp_off(sbase + kc) + c0 + clen]

                if fm:
                    tgt = scr_fm["r" if m == "r" else "p"]
                    for ec in range(8):
                        gd = bank_guard[ec]
                        em.wait("tensor", gd[0], gd[1])
                        wcol = wbase + 128 * ec
                        for kc in range(8):
                            em.op("tensor", _mm(
                                ps[ec][:, 0:512],
                                ar[:, wcol + 1024 * kc: wcol + 1024 * kc + 128],
                                inp_sl(kc), kc == 0, kc == 7),
                                inc="PE" if kc == 7 else None)
                        pe_b = em.cnt["PE"]
                        si, stg_ap = stg_slot()
                        if stg_last[si]:
                            em.wait("scalar", stg_last[si][0], stg_last[si][1])
                        em.wait("scalar", "PE", pe_b)
                        if m == "w":
                            em.op("scalar", lambda e, b=ec, s2=stg_ap: e.activation(
                                s2, ps[b][:, 0:512], AF.Exp))
                            em.op("scalar", lambda e, s2=stg_ap: e.activation(
                                s2, s2, AF.Ln, bias=1.0), inc="SE")
                        else:
                            em.op("scalar", lambda e, b=ec, s2=stg_ap: e.copy(
                                s2, ps[b][:, 0:512]), inc="SE")
                        s_ev = em.cnt["SE"]
                        bank_guard[ec] = ("SE", s_ev)
                        em.wait("sync", "SE", s_ev)
                        stg_last[si] = (f"STG{si}",
                                        store(tgt[ec, :, t0:t0 + 512], stg_ap,
                                              sem=f"STG{si}"))
                else:
                    for sub in range(4):
                        si0, stg0 = stg_slot()
                        si1, stg1 = stg_slot()
                        assert si1 == si0 + 1
                        last = None
                        for nh, stg_ap, si in ((0, stg0, si0), (1, stg1, si1)):
                            b = sub * 2 + nh
                            gd = bank_guard[b]
                            em.wait("tensor", gd[0], gd[1])
                            for kc in range(8):
                                em.op("tensor", _mm(
                                    ps[b][:, 0:512], inp_sl(kc, 128 * sub, 128),
                                    ar[:, wbase + 1024 * kc + 512 * nh:
                                       wbase + 1024 * kc + 512 * nh + 512],
                                    kc == 0, kc == 7),
                                    inc="PE" if kc == 7 else None)
                            pe_b = em.cnt["PE"]
                            if stg_last[si]:
                                em.wait("scalar", stg_last[si][0], stg_last[si][1])
                                em.wait("vector", stg_last[si][0], stg_last[si][1])
                            em.wait("scalar", "PE", pe_b)
                            em.wait("vector", "PE", pe_b)
                            func = {"k": AF.Tanh, "g": AF.Sigmoid}.get(m)
                            if func is not None:
                                em.op("scalar", lambda e, bb=b, s2=stg_ap, f=func:
                                      e.activation(s2, ps[bb][:, 0:512], f), inc="SE")
                                bank_guard[b] = ("SE", em.cnt["SE"])
                                last = ("SE", em.cnt["SE"])
                            else:
                                em.op("vector", lambda e, bb=b, s2=stg_ap: e.tensor_copy(
                                    s2, ps[bb][:, 0:512]), inc="VE")
                                bank_guard[b] = ("VE", em.cnt["VE"])
                                last = ("VE", em.cnt["VE"])
                        full = ar[:, off["stg"] + 512 * si0: off["stg"] + 512 * si0 + 1024]
                        if m in ("a", "b"):
                            ss = ar[:, off["lnst"] + 8:off["lnst"] + 24]
                            sqscr = ar[:, off["sqscr"]:off["sqscr"] + 1024]
                            em.wait("vector", "SE", em.cnt["SE"])
                            em.op("vector", lambda e, f_=full, s2=sqscr: e.tensor_tensor(
                                s2, f_, f_, ALU.mult))
                            em.op("vector", lambda e, s2=ss, q2=sqscr: e.tensor_reduce(
                                s2, q2.rearrange("p (h k) -> p h k", k=64), AX.X, ALU.add))
                            em.op("vector", lambda e, s2=ss: e.tensor_scalar(
                                s2, s2, NORM_EPS2, None, ALU.max), inc="VE")
                            v_ss = em.cnt["VE"]
                            em.wait("scalar", "VE", v_ss)
                            em.op("scalar", lambda e, s2=ss: e.activation(s2, s2, AF.Ln))
                            em.op("scalar", lambda e, s2=ss: e.activation(
                                s2, s2, AF.Exp, scale=-0.5), inc="SE")
                            s_inv = em.cnt["SE"]
                            em.wait("vector", "SE", s_inv)
                            em.op("vector", lambda e, f_=full, s2=ss: e.tensor_tensor(
                                f_, f_, s2.unsqueeze(-1).broadcast_to([128, 16, 64]),
                                ALU.mult), inc="VE")
                            last = ("VE", em.cnt["VE"])
                        trow = 128 * (4 * t4 + sub)
                        em.wait("sync", last[0], last[1])
                        stv2 = (f"STG{si0}",
                                store(scr_tm[m][trow:trow + 128, :], full,
                                      sem=f"STG{si0}"))
                        stg_last[si0] = stv2
                        stg_last[si1] = stv2
                for _i in range(8):
                    inp_prev_guard[sbase + _i] = em.cnt["PE"]
            if t4 == 3:
                pe_wdone[mi] = em.cnt["PE"]
                if mi + 2 < len(weights):
                    load_weight(mi + 2)

        em.barrier()

        run_b = PHASES >= 2
        preset()
        pal("sl_r", 2048)
        pal("sl_p", 2048)
        pal("sl_k", 2048)
        pal("sl_v", 2048)
        pal("sl_a", 2048)
        pal("sl_b", 2048)
        pal("sl_g", 2048)
        pal("pc", 2048)
        pal("t2", 2048)
        pal("tinyB", 384)
        pal("pool", 25 * 1024)
        pal("stgB", 2 * 1024)

        pool_free = list(range(25))
        pool_map = {}

        def palloc(name):
            s = max(pool_free)
            pool_free.remove(s)
            pool_map[name] = s
            return ar[:, off["pool"] + 1024 * s: off["pool"] + 1024 * s + 1024]

        def pap(name):
            s = pool_map[name]
            if s < 0:
                s = -s - 1
                return ar[:, off["pool"] + 1024 * s: off["pool"] + 1024 * s + 2048]
            return ar[:, off["pool"] + 1024 * s: off["pool"] + 1024 * s + 1024]

        def bdalloc(name):
            for s in sorted(pool_free):
                if s + 1 in pool_free:
                    pool_free.remove(s)
                    pool_free.remove(s + 1)
                    pool_map[name] = -s - 1
                    return ar[:, off["pool"] + 1024 * s: off["pool"] + 1024 * s + 2048]
            raise AssertionError(("no adjacent free pair", sorted(pool_free)))

        def pfree(name):
            s = pool_map.pop(name)
            if s < 0:
                s = -s - 1
                pool_free.append(s)
                pool_free.append(s + 1)
            else:
                pool_free.append(s)

        evac_tog = [0]

        def evac_eng():
            evac_tog[0] = (evac_tog[0] + 1) % 3
            return "vector" if evac_tog[0] == 0 else "scalar"

        def ev_copy(dst, src, eng=None):
            eng = eng or evac_eng()
            if eng == "vector":
                v = em.op("vector", lambda e, d=dst, s=src: e.tensor_copy(d, s), inc="VE")
                return ("VE", v)
            v = em.op("scalar", lambda e, d=dst, s=src: e.copy(d, s), inc="SE")
            return ("SE", v)

        def ev_mask(dst, bank, mask64):
            v = em.op("vector", lambda e, d=dst, b=bank, mk=mask64: e.scalar_tensor_tensor(
                d.rearrange("p (a b) -> p a b", b=64),
                ps[b][:, 0:512].rearrange("p (a b) -> p a b", b=64),
                1.0, hbc(mk, 8), ALU.mult, ALU.mult), inc="VE")
            return ("VE", v)

        def tm_sl(name, q, h0, hl, p0=0, pl=128):
            base = off[name] + 1024 * q + 64 * h0
            return ar[p0:p0 + pl, base:base + 64 * hl]

        def rf_sl(h, q, pi):
            base = off["sl_r"] + 256 * (h // 2) + 128 * q + 64 * pi
            return ar[64 * (h % 2):64 * (h % 2) + 64, base:base + 64]

        tinyb = off["tinyB"]
        pm_ap = ar[:, tinyb:tinyb + 32]
        pe_ap2 = ar[:, tinyb + 32:tinyb + 64]
        pcs_ap = ar[:, tinyb + 64:tinyb + 96]
        ein_ap = ar[:, tinyb + 96:tinyb + 128]
        gc_ap = ar[:, tinyb + 128:tinyb + 160]
        gst = ar[:, tinyb + 160:tinyb + 192]

        pq_global = [0]
        bguard = [("VE", 0)] * 8
        wait_all_stores("sync")
        NSCR = min(NSC, B_SC) if run_b else 0
        tfs = {}

        def tf_loads(sci2, pe_guard):
            st = tfs.setdefault(sci2, {})
            t0b = SC * sci2
            if pe_guard is not None:
                em.wait("sync", "PE", pe_guard)
                em.wait("sync", "GE", tfs[sci2 - 1]["g_kb"])
            load(ar[:, off["sl_r"]:off["sl_r"] + 2048],
                 scr_fm["r"][:, :, t0b:t0b + SC].rearrange("c p t -> p c t"), sem="LDB")
            st["ldb_r"] = em.cnt["LDB"]
            for m in ("k", "v", "a", "b"):
                load(ar[:, off["sl_" + m]:off["sl_" + m] + 2048],
                     scr_tm[m][t0b:t0b + SC, :].rearrange("(n p) d -> p n d", p=128),
                     sem="LDB")
            st["ldb_ab"] = em.cnt["LDB"]

        def tf_T1(sci2):
            st = tfs[sci2]
            t0b = SC * sci2
            tb2 = off["tinyB"] + 192 * (sci2 % 2)
            pmA = ar[:, tb2:tb2 + 32]
            peA = ar[:, tb2 + 32:tb2 + 64]
            pcsA = ar[:, tb2 + 64:tb2 + 96]
            einA = ar[:, tb2 + 96:tb2 + 128]
            gcA = ar[:, tb2 + 128:tb2 + 160]
            prev = tfs.get(sci2 - 1)
            slp = ar[:, off["sl_p"]:off["sl_p"] + 2048]
            pc = ar[:, off["pc"]:off["pc"] + 2048]
            t2a = ar[:, off["t2"]:off["t2"] + 2048]
            if prev is not None:
                em.wait("sync", "PE", prev["pe_e2t"])
                em.wait("sync", "VE", prev["v_e2"])
            ldp = load(slp, scr_fm["p"][:, :, t0b:t0b + SC].rearrange("c p t -> p c t"),
                       sem="LDB")
            em.wait("vector", "LDB", ldp)
            if prev is not None:
                em.wait("vector", "PE", prev["pe_e2t"])
                em.wait("vector", "SE", prev["s_eA"])
            for fc in range(NFC):
                em.op("vector", lambda e, f=fc: e.tensor_tensor_scan(
                    pc[:, 256 * f:256 * f + 256], slp[:, 256 * f:256 * f + 256],
                    slp[:, 256 * f:256 * f + 256], 0.0, ALU.add, ALU.bypass),
                    inc="VE" if fc == NFC - 1 else None)
            pc3 = pc.rearrange("p (c s t) -> p c s t", s=4, t=64)
            em.op("vector", lambda e: e.tensor_copy(
                pmA.rearrange("p (c s) -> p c s", s=4),
                pc3[:, :, :, 31:32].squeeze(-1)))
            em.op("vector", lambda e: e.tensor_copy(
                peA.rearrange("p (c s) -> p c s", s=4),
                pc3[:, :, :, 63:64].squeeze(-1)))
            em.op("vector", lambda e: e.memset(pcsA, 0.0), inc="VE")
            em.op("vector", lambda e: e.tensor_copy(
                pcsA.rearrange("p (c s) -> p c s", s=4)[:, :, 1:4],
                peA.rearrange("p (c s) -> p c s", s=4)[:, :, 0:3]))
            em.op("vector", lambda e: e.tensor_tensor(einA, pcsA, pmA, ALU.subtract))
            em.op("vector", lambda e: e.tensor_tensor(gcA, pmA, peA, ALU.subtract),
                  inc="VE")
            v_eg = em.cnt["VE"]
            em.wait("scalar", "VE", v_eg)
            em.op("scalar", lambda e: e.activation(einA, einA, AF.Exp))
            em.op("scalar", lambda e: e.activation(gcA, gcA, AF.Exp), inc="SE")
            for s_ in range(4):
                pmv = pmA.rearrange("p (c s) -> p c s", s=4)[:, :, s_:s_ + 1].broadcast_to(
                    [128, 8, 64])
                pcv = pc.rearrange(
                    "p (c s t) -> p c s t", s=4, t=64)[:, :, s_:s_ + 1, :].squeeze(2)
                em.op("vector", lambda e, a=pcv, b=pmv: e.tensor_tensor(
                    a, a, b, ALU.subtract), inc="VE" if s_ == 3 else None)
            v_d1 = em.cnt["VE"]
            ea_p = bdalloc("Ea")
            em.wait("scalar", "VE", v_d1)
            if prev is not None:
                em.wait("scalar", "VE", prev["v_rhat"])
            em.op("scalar", lambda e, d=ea_p: e.activation(d, pc, AF.Exp, scale=-1.0),
                  inc="SE")
            st["s_eA"] = em.cnt["SE"]
            em.op("vector", lambda e: e.tensor_tensor(slp, pc, slp, ALU.subtract), inc="VE")
            st["v_e2"] = em.cnt["VE"]
            v_e2 = st["v_e2"]
            em.wait("tensor", "VE", v_d1)
            for q2 in range(2):
                for b in (0, 1):
                    gd = bguard[b]
                    em.wait("tensor", gd[0], gd[1])
                for fc in range(NFC):
                    bank = fc // 4
                    qq = fc % 4
                    src = ar[:, off["pc"] + 256 * fc + 128 * q2:
                             off["pc"] + 256 * fc + 128 * q2 + 128]
                    em.op("tensor", _tp(ps[bank][:, 128 * qq:128 * qq + 128], src, IDT,
                                        start=(qq == 0), stop=(qq == 3)),
                          inc="PE" if fc == NFC - 1 else None)
                pe_t = em.cnt["PE"]
                em.wait("vector", "PE", pe_t)
                if q2 == 0 and prev is not None:
                    em.wait("vector", "GE", prev["g_kb"])
                for half in range(2):
                    dst = ar[:, off["t2"] + 1024 * q2 + 512 * half:
                             off["t2"] + 1024 * q2 + 512 * half + 512]
                    em.op("vector", lambda e, d=dst, b=half: e.tensor_copy(
                        d, ps[b][:, 0:512]), inc="VE")
                bguard[0] = ("VE", em.cnt["VE"])
                bguard[1] = ("VE", em.cnt["VE"])
            v_d1t = em.cnt["VE"]
            em.wait("scalar", "VE", v_d1t)
            em.op("scalar", lambda e: e.activation(t2a, t2a, AF.Exp), inc="SE")
            st["s_ept"] = em.cnt["SE"]
            fa_p = bdalloc("Fa")
            em.wait("tensor", "VE", v_e2)
            for q2 in range(2):
                for b in (0, 1):
                    gd = bguard[b]
                    em.wait("tensor", gd[0], gd[1])
                for fc in range(NFC):
                    bank = fc // 4
                    qq = fc % 4
                    src = ar[:, off["sl_p"] + 256 * fc + 128 * q2:
                             off["sl_p"] + 256 * fc + 128 * q2 + 128]
                    em.op("tensor", _tp(ps[bank][:, 128 * qq:128 * qq + 128], src, IDT,
                                        start=(qq == 0), stop=(qq == 3)),
                          inc="PE" if fc == NFC - 1 else None)
                pe_t = em.cnt["PE"]
                em.wait("vector", "PE", pe_t)
                for half in range(2):
                    dst = fa_p[:, 1024 * q2 + 512 * half:1024 * q2 + 512 * half + 512]
                    em.op("vector", lambda e, d=dst, b=half: e.tensor_copy(
                        d, ps[b][:, 0:512]), inc="VE")
                bguard[0] = ("VE", em.cnt["VE"])
                bguard[1] = ("VE", em.cnt["VE"])
            st["pe_e2t"] = em.cnt["PE"]
            v_e2t = em.cnt["VE"]
            em.wait("scalar", "VE", v_e2t)
            em.op("scalar", lambda e, d=fa_p: e.activation(d, d, AF.Exp, scale=-1.0),
                  inc="SE")
            st["s_fat"] = em.cnt["SE"]
            st["ea"] = ea_p
            st["fa"] = fa_p

        def tf_gload(sci2):
            st = tfs[sci2]
            t0b = SC * sci2
            if sci2 > 0:
                em.wait("sync", "VE", tfs[sci2 - 1]["v_gate"])
            load(ar[:, off["sl_g"]:off["sl_g"] + 2048],
                 scr_tm["g"][t0b:t0b + SC, :].rearrange("(n p) d -> p n d", p=128),
                 sem="LDB")
            st["ldb_g"] = em.cnt["LDB"]

        def tf_T2(sci2):
            st = tfs[sci2]
            t2a = ar[:, off["t2"]:off["t2"] + 2048]
            em.wait("vector", "LDB", st["ldb_r"])
            em.wait("vector", "SE", st["s_eA"])
            slr = ar[:, off["sl_r"]:off["sl_r"] + 2048]
            em.op("vector", lambda e, a=st["ea"]: e.tensor_tensor(slr, slr, a, ALU.mult),
                  inc="VE")
            st["v_rhat"] = em.cnt["VE"]
            em.wait("gpsimd", "SE", st["s_ept"])
            em.wait("gpsimd", "LDB", st["ldb_ab"])
            slk = ar[:, off["sl_k"]:off["sl_k"] + 2048]
            slb = ar[:, off["sl_b"]:off["sl_b"] + 2048]
            em.op("gpsimd", lambda e: e.tensor_tensor(slk, slk, t2a, ALU.mult))
            em.op("gpsimd", lambda e: e.tensor_tensor(slb, slb, t2a, ALU.mult), inc="GE")
            st["g_kb"] = em.cnt["GE"]
            em.wait("vector", "SE", st["s_fat"])
            em.wait("vector", "LDB", st["ldb_ab"])
            sla = ar[:, off["sl_a"]:off["sl_a"] + 2048]
            em.op("vector", lambda e, f=st["fa"]: e.tensor_tensor(sla, sla, f, ALU.mult),
                  inc="VE")
            em.wait("vector", "GE", st["g_kb"])
            em.op("vector", lambda e: e.nop(nofuse=True), inc="VE")
            st["v_tf_done"] = em.cnt["VE"]
            pfree("Ea")
            pfree("Fa")

        for sci in range(NSCR):
            t0 = SC * sci
            par = sci % 2
            tinyb = off["tinyB"] + 192 * par
            pm_ap = ar[:, tinyb:tinyb + 32]
            pe_ap2 = ar[:, tinyb + 32:tinyb + 64]
            pcs_ap = ar[:, tinyb + 64:tinyb + 96]
            ein_ap = ar[:, tinyb + 96:tinyb + 128]
            gc_ap = ar[:, tinyb + 128:tinyb + 160]
            gst = ar[:, tinyb + 160:tinyb + 192]
            if sci == 0:
                tf_loads(0, None)
                tf_T1(0)
                tf_T2(0)
            tf_gload(sci)
            ldb_tot = tfs[sci]["ldb_ab"]
            v_tf_done = tfs[sci]["v_tf_done"]
            if debug:
                em.wait("sync", "VE", v_tf_done)
                for nmx, dbt in (("sl_k", dbg_kt), ("sl_b", dbg_bt), ("sl_a", dbg_at)):
                    dv2 = store(dbt[sci], ar[:, off[nmx]:off[nmx] + 2048], sem="STG3")
                    em.wait("sync", "STG3", dv2)
                em.wait("vector", "STG3", em.cnt["STG3"])

            ev_done = {}
            otT = {}
            def psl(name, h, pi):
                base = off["pool"] + 1024 * pool_map[name] + 64 * h
                return ar[64 * pi:64 * pi + 64, base:base + 64]

            def fpair(name, h):
                base = off["pool"] + 1024 * pool_map[name] + 128 * (h // 2)
                return ar[64 * (h % 2):64 * (h % 2) + 64, base:base + 128]

            def rpair(h, q2):
                base = off["sl_r"] + 256 * (h // 2) + 128 * q2
                return ar[64 * (h % 2):64 * (h % 2) + 64, base:base + 128]

            def bdb(name, h):
                s = pool_map[name]
                assert s < 0
                base = off["pool"] + 1024 * (-s - 1) + 128 * h
                return ar[:, base:base + 128]

            def bdc(name, h, pi):
                s = pool_map[name]
                base = off["pool"] + 1024 * (-s - 1) + 128 * h + 64 * pi
                return ar[64 * pi:64 * pi + 64, base:base + 64]

            def wait_ev(eng, *names):
                for nm in names:
                    for sem_, cnt_ in ev_done[nm]:
                        em.wait(eng, sem_, cnt_)

            def prod_call(b0, lname, rhs_fn):
                for b in range(b0, b0 + 4):
                    gd = bguard[b]
                    em.wait("tensor", gd[0], gd[1])
                for h in list(range(0, 16, 2)) + [None] + list(range(1, 16, 2)):
                    if h is None:
                        em.op("tensor", lambda e: e.drain())
                        continue
                    em.op("tensor", _mm(
                        ps[b0 + h // 4][:, 128 * (h % 4):128 * (h % 4) + 128],
                        fpair(lname, h), rhs_fn(h), True, True),
                        inc="PE" if h == 15 else None)
                return em.cnt["PE"]

            def ev_bd(name, b0, pe_v, mask_bd):
                em.wait("vector", "PE", pe_v)
                dstp = bdalloc(name)
                gs = []
                for i in range(4):
                    v = em.op("vector", lambda e, d=dstp[:, 512 * i:512 * i + 512],
                              b=b0 + i, mk=mask_bd: e.scalar_tensor_tensor(
                                  d.rearrange("p (a c) -> p a c", c=128),
                                  ps[b][:, 0:512].rearrange("p (a c) -> p a c", c=128),
                                  1.0, mk.unsqueeze(1).broadcast_to([128, 4, 128]),
                                  ALU.mult, ALU.mult), inc="VE")
                    gs.append(("VE", v))
                    bguard[b0 + i] = ("VE", v)
                ev_done[name] = gs
                return dstp

            def ev_diag(name, b0, pe_v, mask64):
                em.wait("vector", "PE", pe_v)
                dstp = palloc(name)
                gs = []
                for b in range(4):
                    for pi in range(2):
                        src = ps[b0 + b][64 * pi:64 * pi + 64, 0:512].rearrange(
                            "p (a two c) -> p a two c", two=2, c=64)[
                                :, :, pi:pi + 1, :].squeeze(2)
                        dst = dstp[64 * pi:64 * pi + 64,
                                   256 * b:256 * b + 256].rearrange(
                            "p (a c) -> p a c", c=64)
                        mk = mask64[64 * pi:64 * pi + 64, :]
                        v = em.op("vector", lambda e, d=dst, s=src, m=mk:
                                  e.scalar_tensor_tensor(
                                      d, s, 1.0,
                                      m.unsqueeze(1).broadcast_to([64, 4, 64]),
                                      ALU.mult, ALU.mult), inc="VE")
                    gs.append(("VE", v))
                    bguard[b0 + b] = ("VE", v)
                ev_done[name] = gs
                return dstp

            def mm_bd(b0, lname, rname):
                wait_ev("tensor", lname, rname)
                for b in range(b0, b0 + 4):
                    gd = bguard[b]
                    em.wait("tensor", gd[0], gd[1])
                for h in range(16):
                    em.op("tensor", _mm(
                        ps[b0 + h // 4][:, 128 * (h % 4):128 * (h % 4) + 128],
                        bdb(lname, h), bdb(rname, h), True, True),
                        inc="PE" if h == 15 else None)
                return em.cnt["PE"]

            def ev_bd_copy(name, b0, pe_v):
                em.wait("vector", "PE", pe_v)
                em.wait("scalar", "PE", pe_v)
                dstp = bdalloc(name)
                gs = []
                for i in range(4):
                    g = ev_copy(dstp[:, 512 * i:512 * i + 512], ps[b0 + i][:, 0:512])
                    gs.append(g)
                    bguard[b0 + i] = g
                ev_done[name] = gs
                return dstp

            def ev_bd_add(name, b0, pe_v, addname):
                em.wait("vector", "PE", pe_v)
                dstp = bdalloc(name)
                gs = []
                for i in range(4):
                    v = em.op("vector", lambda e, d=dstp[:, 512 * i:512 * i + 512],
                              b=b0 + i,
                              a_=pap(addname)[:, 512 * i:512 * i + 512]:
                              e.tensor_tensor(d, ps[b][:, 0:512], a_, ALU.add),
                              inc="VE")
                    gs.append(("VE", v))
                    bguard[b0 + i] = ("VE", v)
                ev_done[name] = gs
                return dstp

            bank_tog = [0]

            def nb():
                b0 = 4 * bank_tog[0]
                bank_tog[0] ^= 1
                return b0

            def phase1_gen(q):
                def N(n):
                    return n + str(q)

                for nm, src_name in (("afP", "sl_a"), ("bfP", "sl_b"), ("kfP", "sl_k")):
                    b0 = nb()
                    dstp = palloc(N(nm))
                    em.wait("tensor", "VE", v_tf_done if nm == "afP" else em.cnt["VE"])
                    for b in (b0 + 2, b0 + 3):
                        gd = bguard[b]
                        em.wait("tensor", gd[0], gd[1])
                    for hp in range(8):
                        bank = b0 + 2 + hp // 4
                        qq = hp % 4
                        src = tm_sl(src_name, q, 2 * hp, 2)
                        em.op("tensor", _tp(ps[bank][:, 128 * qq:128 * qq + 128], src, IDT,
                                            start=(qq == 0), stop=(qq == 3)),
                              inc="PE" if hp == 7 else None)
                    pe_t = em.cnt["PE"]
                    em.wait("vector", "PE", pe_t)
                    em.wait("scalar", "PE", pe_t)
                    g1 = ev_copy(dstp[:, 0:512], ps[b0 + 2][:, 0:512])
                    g2 = ev_copy(dstp[:, 512:1024], ps[b0 + 3][:, 0:512])
                    bguard[b0 + 2] = g1
                    bguard[b0 + 3] = g2
                    yield
                ve_fm = em.cnt["VE"]
                se_fm = em.cnt["SE"]

                em.wait("tensor", "VE", ve_fm)
                em.wait("tensor", "SE", se_fm)
                b0 = nb()
                p_ = prod_call(b0, N("bfP"), lambda h: fpair(N("afP"), h))
                ev_bd(N("Wu"), b0, p_, A("bdU"))
                yield
                b0 = nb()
                p_ = prod_call(b0, N("afP"), lambda h: fpair(N("bfP"), h))
                ev_bd(N("Wup"), b0, p_, A("bdL"))
                yield
                b0 = nb()
                p_ = prod_call(b0, N("kfP"), lambda h: fpair(N("afP"), h))
                ev_bd(N("Hu"), b0, p_, A("bdU"))
                yield
                b0 = nb()
                p_ = prod_call(b0, N("bfP"), lambda h: rpair(h, q))
                ev_diag(f"Qu{q}", b0, p_, A("maskUI"))
                yield
                b0 = nb()
                p_ = prod_call(b0, N("kfP"), lambda h: rpair(h, q))
                ev_diag(f"SA{q}", b0, p_, A("maskUI"))
                pfree(N("afP"))
                pfree(N("bfP"))
                pfree(N("kfP"))
                yield

                if B_STAGE < 3:
                    for nmf in list(pool_map):
                        pfree(nmf)
                    return

                b0 = nb()
                p_ = mm_bd(b0, N("Wup"), N("Wu"))
                ev_bd_copy(N("L2"), b0, p_)
                yield
                b0 = nb()
                pe_l2p = mm_bd(b0, N("Wu"), N("Wup"))
                ev_bd_copy(N("L2p"), b0, pe_l2p)
                em.wait("vector", "PE", pe_l2p)
                em.op("vector", lambda e, d=pap(N("Wup")): e.tensor_tensor(
                    d.rearrange("p (a c) -> p a c", c=128),
                    d.rearrange("p (a c) -> p a c", c=128),
                    IDT.unsqueeze(1).broadcast_to([128, 16, 128]), ALU.add), inc="VE")
                pool_map[N("M1p")] = pool_map.pop(N("Wup"))
                ev_done[N("M1p")] = [("VE", em.cnt["VE"])]
                pfree(N("Wu"))
                yield

                b0 = nb()
                hv_p = palloc(f"HV{q}")
                wait_ev("tensor", N("Hu"))
                em.wait("tensor", "LDB", ldb_tot)
                for b in (b0, b0 + 1):
                    gd = bguard[b]
                    em.wait("tensor", gd[0], gd[1])
                for h in range(16):
                    em.op("tensor", _mm(
                        ps[b0 + h // 8][:, 64 * (h % 8):64 * (h % 8) + 64],
                        bdb(N("Hu"), h), tm_sl("sl_v", q, h, 1), True, True),
                        inc="PE" if h == 15 else None)
                pe_hv = em.cnt["PE"]
                em.wait("vector", "PE", pe_hv)
                em.wait("scalar", "PE", pe_hv)
                g1 = ev_copy(hv_p[:, 0:512], ps[b0][:, 0:512])
                g2 = ev_copy(hv_p[:, 512:1024], ps[b0 + 1][:, 0:512])
                bguard[b0] = g1
                bguard[b0 + 1] = g2
                ev_done[f"HV{q}"] = [g1, g2]
                pfree(N("Hu"))
                yield

                chain = [("L4", "L2p", "L2", None), ("L4p", "L2", "L2p", ("L2p",)),
                         ("M2p", "L2", "M1p", ("L2", "M1p")),
                         ("L8", "L4p", "L4", None), ("L8p", "L4", "L4p", ("L4p",)),
                         ("M3p", "L4", "M2p", ("L4", "M2p")),
                         ("L16", "L8p", "L8", None), ("L16p", "L8", "L8p", ("L8p",)),
                         ("M4p", "L8", "M3p", ("L8", "M3p")),
                         ("L32", "L16p", "L16", ("L16p",)),
                         ("M5p", "L16", "M4p", ("L16", "M4p")),
                         ("M6p", "L32", "M5p", ("L32", "M5p"))]
                for out, ln_, rn_, frees in chain:
                    b0 = nb()
                    p_ = mm_bd(b0, N(ln_), N(rn_))
                    if out.startswith("M"):
                        ev_bd_add(N(out), b0, p_, N(rn_))
                    else:
                        ev_bd_copy(N(out), b0, p_)
                    if frees:
                        for f_ in frees:
                            pfree(N(f_))
                    yield
                b0 = nb()
                wait_ev("tensor", N("M6p"))
                for b in range(b0, b0 + 4):
                    gd = bguard[b]
                    em.wait("tensor", gd[0], gd[1])
                for h in range(16):
                    em.op("tensor", _tp(
                        ps[b0 + h // 4][:, 128 * (h % 4):128 * (h % 4) + 128],
                        bdb(N("M6p"), h), IDT, start=(h % 4 == 0), stop=(h % 4 == 3)),
                        inc="PE" if h == 15 else None)
                p_ = em.cnt["PE"]
                ev_bd_copy(f"M{q}", b0, p_)
                pfree(N("M6p"))

            gens = [phase1_gen(0), phase1_gen(1)] if B_STAGE >= 2 else []
            alive = [True] * len(gens)
            for _ in range(11 if gens else 0):
                try:
                    next(gens[0])
                except StopIteration:
                    alive[0] = False
            gi = 1
            while any(alive):
                if alive[gi]:
                    try:
                        next(gens[gi])
                    except StopIteration:
                        alive[gi] = False
                if len(gens) == 2:
                    gi = 1 - gi

            for q in range(2 if B_STAGE >= 4 else 0):
                ch0 = 2 * q
                hv_p = pap(f"HV{q}")
                cf_p = palloc("cf")
                zt_p = [palloc("Zt0"), palloc("Zt1")]
                skin_p = [palloc("Skin0"), palloc("Skin1")]
                stmp_p = palloc("stmp")

                def hv_sl(h, pi):
                    base = off["pool"] + 1024 * pool_map[f"HV{q}"] + 64 * h
                    return ar[64 * pi:64 * pi + 64, base:base + 64]

                def zt_sl(h, pi):
                    base = off["pool"] + 1024 * pool_map["Zt" + str(pi)] + 64 * h
                    return ar[64 * pi:64 * pi + 64, base:base + 64]

                def cf_sl(h, pi):
                    base = (off["pool"] + 1024 * pool_map["cf"] + 512 * pi
                            + 64 * (h // 2))
                    return ar[64 * (h % 2):64 * (h % 2) + 64, base:base + 64]

                def skin_sl(h, pi):
                    base = (off["pool"] + 1024 * pool_map["Skin" + str(pi)]
                            + 64 * (h // 2))
                    return ar[64 * (h % 2):64 * (h % 2) + 64, base:base + 64]

                pe_qstart = em.cnt["PE"]
                for pi in range(2):
                    gd = bguard[6]
                    em.wait("tensor", gd[0], gd[1])
                    wait_ev("tensor", f"M{q}")
                    for h in range(16):
                        em.op("tensor", _mm(
                            ps[6][64 * (h % 2):64 * (h % 2) + 64,
                                  64 * (h // 2):64 * (h // 2) + 64],
                            tm_sl("sl_a", q, h, 1, 64 * pi, 64), bdc(f"M{q}", h, pi),
                            (h // 2 == 0), (h // 2 == 7)),
                            inc="PE" if h == 15 else None)
                    pe_cf = em.cnt["PE"]
                    em.wait("vector", "PE", pe_cf)
                    gcf = ev_copy(cf_p[:, 512 * pi:512 * pi + 512], ps[6][:, 0:512],
                                  eng="vector")
                    bguard[6] = gcf
                gcf = gcf
                for pi in range(2):
                    ch = ch0 + pi
                    em.wait("vector", "PE", pe_qstart)
                    ein_b = ein_ap.rearrange("p (c s) -> p c s", s=4)[
                        :, :, ch:ch + 1].broadcast_to([128, 8, 64])
                    em.op("vector", lambda e, d=skin_p[pi], eb=ein_b: e.tensor_tensor(
                        d[:, 0:512].rearrange("p (a b) -> p a b", b=64),
                        A("sk").rearrange("p (a b) -> p a b", b=64), eb, ALU.mult), inc="VE")
                    v_skin = em.cnt["VE"]
                    if debug:
                        em.wait("sync", "VE", v_skin)
                        dvsk = store(dbg_skin[4 * sci + ch], skin_p[pi][:, 0:512],
                                     sem="STG3")
                        em.wait("vector", "STG3", dvsk)
                    for b in (4, 5):
                        gd = bguard[b]
                        em.wait("tensor", gd[0], gd[1])
                    em.wait("tensor", gcf[0], gcf[1])
                    for h in range(16):
                        b = 4 + h // 8
                        oap = ps[b][64 * pi:64 * pi + 64, 64 * (h % 8):64 * (h % 8) + 64]
                        em.op("tensor", _mm(oap, bdc(f"M{q}", h, pi), hv_sl(h, pi),
                                            (h % 8 == 0), False))
                    em.op("tensor", lambda e: e.drain())
                    em.wait("tensor", "VE", v_skin)
                    for h in list(range(0, 16, 2)) + [None] + list(range(1, 16, 2)):
                        if h is None:
                            em.op("tensor", lambda e: e.drain())
                            continue
                        b = 4 + h // 8
                        oap = ps[b][64 * pi:64 * pi + 64, 64 * (h % 8):64 * (h % 8) + 64]
                        em.op("tensor", _mm(oap, cf_sl(h, pi), skin_sl(h, pi),
                                            False, (h % 8 == 7)),
                              inc="PE" if h == 15 else None)
                    em.op("tensor", lambda e: e.drain())
                    pe_zt = em.cnt["PE"]
                    em.wait("vector", "PE", pe_zt)
                    em.wait("scalar", "PE", pe_zt)
                    g1 = ev_copy(zt_p[pi][64 * pi:64 * pi + 64, 0:512],
                                 ps[4][64 * pi:64 * pi + 64, 0:512])
                    g2 = ev_copy(zt_p[pi][64 * pi:64 * pi + 64, 512:1024],
                                 ps[5][64 * pi:64 * pi + 64, 0:512])
                    bguard[4] = g1
                    bguard[5] = g2
                    gd = bguard[7]
                    em.wait("tensor", gd[0], gd[1])
                    em.wait("tensor", g1[0], g1[1])
                    em.wait("tensor", g2[0], g2[1])
                    for h in range(16):
                        oap = ps[7][64 * (h % 2):64 * (h % 2) + 64,
                                    64 * (h // 2):64 * (h // 2) + 64]
                        em.op("tensor", _mm(oap, tm_sl("sl_b", q, h, 1, 64 * pi, 64),
                                            zt_sl(h, pi), (h // 2 == 0), False))
                        em.op("tensor", _mm(oap, tm_sl("sl_k", q, h, 1, 64 * pi, 64),
                                            tm_sl("sl_v", q, h, 1, 64 * pi, 64),
                                            False, (h // 2 == 7)),
                              inc="PE" if h == 15 else None)
                    pe_sn = em.cnt["PE"]
                    if debug:
                        em.wait("vector", "PE", pe_sn)
                        bq = ar[:, off["t2"]:off["t2"] + 512]
                        em.op("vector", lambda e, b=bq: e.tensor_copy(b, ps[7][:, 0:512]),
                              inc="VE")
                        v_d7 = em.cnt["VE"]
                        em.wait("sync", "VE", v_d7)
                        dv7 = store(dbg_ps7[4 * sci + ch], bq, sem="STG3")
                        em.wait("vector", "STG3", dv7)
                    gc_b = gc_ap.rearrange("p (c s) -> p c s", s=4)[
                        :, :, ch:ch + 1].broadcast_to([128, 8, 64])
                    em.wait("vector", "PE", pe_sn)
                    em.op("vector", lambda e, s_=stmp_p, sp=skin_p[pi]: e.tensor_tensor(
                        s_[:, 0:512], ps[7][:, 0:512], sp[:, 0:512], ALU.add))
                    em.op("vector", lambda e, gb=gc_b, s_=stmp_p: e.tensor_tensor(
                        A("sk").rearrange("p (a b) -> p a b", b=64),
                        s_[:, 0:512].rearrange("p (a b) -> p a b", b=64), gb, ALU.mult),
                        inc="VE")
                    v_sk = em.cnt["VE"]
                    bguard[7] = ("VE", v_sk)
                    if debug:
                        em.wait("sync", "VE", v_sk)
                        dvs = store(dbg_st[4 * sci + ch], stmp_p[:, 0:512], sem="STG3")
                        em.wait("sync", "STG3", dvs)
                        dvk = store(dbg_sk[4 * sci + ch], A("sk"), sem="STG3")
                        em.wait("sync", "STG3", dvk)
                        dvz = store(dbg_zt[4 * sci + ch], zt_p[pi], sem="STG3")
                        em.wait("vector", "STG3", dvz)
                    if pi == 0:
                        for b in (2, 3):
                            gd = bguard[b]
                            em.wait("tensor", gd[0], gd[1])
                    first = set()
                    for h in list(range(0, 16, 2)) + [None] + list(range(1, 16, 2)):
                        if h is None:
                            em.op("tensor", lambda e: e.drain())
                            continue
                        b = 2 + h // 8
                        oap = ps[b][64 * pi:64 * pi + 64, 64 * (h % 8):64 * (h % 8) + 64]
                        st_ = b not in first
                        first.add(b)
                        em.op("tensor", _mm(oap, rf_sl(h, q, pi), skin_sl(h, pi),
                                            st_, False))
                    em.op("tensor", lambda e: e.drain())
                    for h in range(16):
                        b = 2 + h // 8
                        oap = ps[b][64 * pi:64 * pi + 64, 64 * (h % 8):64 * (h % 8) + 64]
                        em.op("tensor", _mm(oap, psl(f"Qu{q}", h, pi), zt_sl(h, pi),
                                            False, False))
                        em.op("tensor", _mm(oap, psl(f"SA{q}", h, pi),
                                            tm_sl("sl_v", q, h, 1, 64 * pi, 64),
                                            False, (h % 8 == 7)),
                              inc="PE" if h == 15 else None)
                pe_o = em.cnt["PE"]
                pfree(f"HV{q}")
                pfree("cf")
                pfree("Zt0")
                pfree("Zt1")
                pfree("Skin0")
                pfree("Skin1")
                pfree("stmp")
                pfree(f"Qu{q}")
                pfree(f"SA{q}")
                pfree(f"M{q}")

                if B_STAGE < 5:
                    continue
                ot_p = palloc(f"Ot{q}")
                sq_p = palloc(f"sqB{q}")
                otT[q] = (ot_p, sq_p)
                em.wait("vector", "PE", pe_o)
                em.op("vector", lambda e, d=ot_p: e.tensor_copy(d[:, 0:512], ps[2][:, 0:512]))
                em.op("vector", lambda e, d=ot_p: e.tensor_copy(d[:, 512:1024], ps[3][:, 0:512]),
                      inc="VE")
                v_ot = em.cnt["VE"]
                bguard[2] = ("VE", v_ot)
                bguard[3] = ("VE", v_ot)
                if debug:
                    em.wait("sync", "VE", v_ot)
                    trow_d = t0 + 128 * q
                    dv = store(dbg_o[trow_d:trow_d + 128, :], ot_p, sem="STG3")
                    em.wait("vector", "STG3", dv)
                if q == 1 and sci + 1 < NSCR:
                    tf_loads(sci + 1, pe_o)
                    tf_T1(sci + 1)
                    tf_T2(sci + 1)

            for q in range(2 if B_STAGE >= 5 else 0):
                ot_p, sq_p = otT[q]
                mus = gst[:, 0:16]
                qs = ar[:, tinyb + 160 + 16:tinyb + 160 + 32]
                em.op("vector", lambda e, d=ot_p, s=sq_p: e.tensor_tensor(s, d, d, ALU.mult))
                em.op("vector", lambda e, d=ot_p, m_=mus: e.tensor_reduce(
                    m_, d.rearrange("p (h k) -> p h k", k=64), AX.X, ALU.add))
                em.op("vector", lambda e, s=sq_p, qq=qs: e.tensor_reduce(
                    qq, s.rearrange("p (h k) -> p h k", k=64), AX.X, ALU.add))
                em.op("vector", lambda e, m_=mus: e.tensor_scalar(m_, m_, 1.0 / KH, None,
                                                                  ALU.mult))
                em.op("vector", lambda e, s=sq_p, m_=mus: e.tensor_tensor(
                    s[:, 0:16], m_, m_, ALU.mult))
                em.op("vector", lambda e, qq=qs, s=sq_p: e.scalar_tensor_tensor(
                    qq, qq, 1.0 / KH, s[:, 0:16], ALU.mult, ALU.subtract), inc="VE")
                v_gn1 = em.cnt["VE"]
                em.wait("scalar", "VE", v_gn1)
                em.op("scalar", lambda e, qq=qs: e.activation(qq, qq, AF.Ln, bias=GN_EPS))
                em.op("scalar", lambda e, qq=qs: e.activation(qq, qq, AF.Exp, scale=-0.5),
                      inc="SE")
                s_rstd = em.cnt["SE"]
                em.wait("vector", "SE", s_rstd)
                em.op("vector", lambda e, m_=mus, qq=qs: e.scalar_tensor_tensor(
                    m_, m_, -1.0, qq, ALU.mult, ALU.mult))
                em.op("vector", lambda e, d=ot_p, qq=qs: e.tensor_tensor(
                    d.rearrange("p (h k) -> p h k", k=64),
                    d.rearrange("p (h k) -> p h k", k=64),
                    qq.unsqueeze(-1).broadcast_to([128, 16, 64]), ALU.mult))
                em.op("vector", lambda e, d=ot_p, m_=mus: e.tensor_tensor(
                    d.rearrange("p (h k) -> p h k", k=64),
                    d.rearrange("p (h k) -> p h k", k=64),
                    m_.unsqueeze(-1).broadcast_to([128, 16, 64]), ALU.add))
                em.op("vector", lambda e, d=ot_p: e.tensor_tensor(d, d, A("gnw"), ALU.mult))
                em.op("vector", lambda e, d=ot_p: e.tensor_tensor(d, d, A("gnb"), ALU.add))
                gsl = ar[:, off["sl_g"] + 1024 * q:off["sl_g"] + 1024 * q + 1024]
                em.wait("vector", "LDB", tfs[sci]["ldb_g"])
                em.op("vector", lambda e, d=ot_p, g_=gsl: e.tensor_tensor(
                    d, d, g_, ALU.mult), inc="VE")
                v_ng = em.cnt["VE"]
                if q == 1:
                    tfs[sci]["v_gate"] = v_ng
                em.wait("tensor", "VE", v_ng)
                for b in (0, 1):
                    gd = bguard[b]
                    em.wait("tensor", gd[0], gd[1])
                for fc in range(NFC):
                    bank = fc // 4
                    qq = fc % 4
                    em.op("tensor", _tp(ps[bank][:, 128 * qq:128 * qq + 128],
                                        ot_p[:, 128 * fc:128 * fc + 128], IDT,
                                        start=(qq == 0), stop=(qq == 3)),
                          inc="PE" if fc == NFC - 1 else None)
                pe_ngt = em.cnt["PE"]
                em.wait("vector", "PE", pe_ngt)
                sgB = ar[:, off["stgB"]:off["stgB"] + 2048]
                pqg = pq_global[0]
                sslot = pqg % 2
                sgap = ar[:, off["stgB"] + 1024 * sslot:off["stgB"] + 1024 * sslot + 1024]
                if pqg >= 2:
                    em.wait("vector", f"STG{sslot}", ng_store[pqg - 2])
                em.op("vector", lambda e, s=sgap: e.tensor_copy(s[:, 0:512], ps[0][:, 0:512]))
                em.op("vector", lambda e, s=sgap: e.tensor_copy(s[:, 512:1024], ps[1][:, 0:512]),
                      inc="VE")
                v_ngs = em.cnt["VE"]
                bguard[0] = ("VE", v_ngs)
                bguard[1] = ("VE", v_ngs)
                em.wait("sync", "VE", v_ngs)
                trow = t0 + 128 * q
                ng_store[pqg] = store(
                    scr_ng[:, :, trow:trow + 128].rearrange("c p t -> p c t"),
                    sgap.rearrange("p (c t) -> p c t", t=128), sem=f"STG{sslot}")
                pq_global[0] += 1
                pfree(f"Ot{q}")
                pfree(f"sqB{q}")
            if B_STAGE >= 5:
                assert len(pool_free) == 25, (len(pool_free), pool_map)

        em.barrier()

        run_c = PHASES >= 3
        preset()
        pal("woD", 8192)
        pal("fwD", 8192)
        pal("wpD", 2048)
        pal("ngt", 2 * 1024)
        pal("xct", 2 * 1024)
        pal("x2t", 2 * 1024)
        pal("x2f", 2 * 1024)
        pal("projt", 512)
        pal("retf", 1024)
        pal("stgC", 2 * 1024)
        pal("lnstC", 64)
        pal("sqC", 1024)

        wait_all_stores("sync")
        ldv = None
        for kc in range(8):
            ldv = load(ar[:, off["woD"] + 1024 * kc:off["woD"] + 1024 * kc + 1024],
                       wo_T[128 * kc:128 * kc + 128, :])
        load(ar[:, off["wpD"]:off["wpD"] + 2048],
             wp_T.rearrange("(c p) r -> p c r", p=128))
        ld_wo = em.cnt["LD"]
        for kc in range(8):
            ldv = load(ar[:, off["fwD"] + 1024 * kc:off["fwD"] + 1024 * kc + 1024],
                       ffnw_T[128 * kc:128 * kc + 128, :])
        ld_cw = em.cnt["LD"]

        bguardC = [("VE", 0)] * 8
        st_out = {}
        pe_woT = {}
        v_x2T = {}
        v_ywgT = {}
        ldxT = {}
        x2aT = {}
        ywgT = {}

        def emit_loads(tt):
            slot = tt % 2
            ng_ap = ar[:, off["ngt"] + 1024 * slot:off["ngt"] + 1024 * slot + 1024]
            xc_ap = ar[:, off["xct"] + 1024 * slot:off["xct"] + 1024 * slot + 1024]
            if tt >= 2:
                em.wait("sync", "PE", pe_tp2T[tt - 2])
                em.wait("sync", "SE", s_xhT[tt - 2])
                em.wait("sync", "VE", v_x2T[tt - 2])
            load(ng_ap.rearrange("p (c t) -> p c t", t=128),
                 scr_ng[:, :, 128 * tt:128 * tt + 128].rearrange("c p t -> p c t"),
                 sem=f"LDX{slot}")
            ldxT[tt] = load(xc_ap, x_in[128 * tt:128 * tt + 128, :], sem=f"LDX{slot}")

        pe_ffnT = {}
        v_rexpT = {}
        pe_ywT = {}
        pe_tp2T = {}
        s_xhT = {}

        def stageR(tt):
            slot = tt % 2
            rexp = ar[:, off["retf"]:off["retf"] + 15]
            em.wait("tensor", "VE", v_rexpT[tt])
            gd = bguardC[6]
            em.wait("tensor", gd[0], gd[1])
            em.op("tensor", _tp(ps[6][0:15, 384:512], rexp, IDT, True, True), inc="PE")
            pe_rt = em.cnt["PE"]
            em.wait("vector", "PE", pe_rt)
            retf = ar[0:15, off["retf"] + 128:off["retf"] + 256]
            em.op("vector", lambda e, r=retf: e.tensor_copy(r, ps[6][0:15, 384:512]),
                  inc="VE")
            v_retf = em.cnt["VE"]
            bguardC[6] = ("VE", v_retf)
            ywg = ar[0:15, off["retf"] + 256 + 128 * slot:off["retf"] + 384 + 128 * slot]
            ywgT[tt] = ywg
            em.wait("vector", "PE", pe_ywT[tt])
            em.op("vector", lambda e, y=ywg, r=retf: e.tensor_tensor(
                y, ps[7][0:15, 0:128], r, ALU.mult), inc="VE")
            v_ywg = em.cnt["VE"]
            v_ywgT[tt] = v_ywg
            bguardC[7] = ("VE", v_ywg)

        def lateC_mm(tt):
            slot = tt % 2
            x2f_ap = ar[:, off["x2f"] + 1024 * slot:off["x2f"] + 1024 * slot + 1024]
            if tt == 0:
                em.wait("tensor", "LD", ld_cw)
            em.wait("tensor", "VE", v_ywgT[tt])
            for b in (4, 5):
                gd = bguardC[b]
                em.wait("tensor", gd[0], gd[1])
            for kc in range(8):
                for nh in range(2):
                    em.op("tensor", _mm(
                        ps[4 + nh][:, 0:512], x2f_ap[:, 128 * kc:128 * kc + 128],
                        ar[:, off["fwD"] + 1024 * kc + 512 * nh:
                           off["fwD"] + 1024 * kc + 512 * nh + 512],
                        kc == 0, False))
            for nh in range(2):
                em.op("tensor", _mm(ps[4 + nh][:, 0:512], ywgT[tt],
                                    ar[0:15, off["wu"] + 512 * nh:off["wu"] + 512 * nh + 512],
                                    False, True), inc="PE" if nh == 1 else None)
            pe_ffnT[tt] = em.cnt["PE"]

        def lateC_fin(tt):
            slot = tt % 2
            x2_ap = x2aT[tt]
            em.wait("vector", "PE", pe_ffnT[tt])
            if tt >= 2:
                em.wait("vector", st_out[tt - 2][0], st_out[tt - 2][1])
            sgap = ar[:, off["stgC"] + 1024 * slot:off["stgC"] + 1024 * slot + 1024]
            for nh in range(2):
                em.op("vector", lambda e, n=nh, s=sgap, x=x2_ap: e.tensor_tensor(
                    s[:, 512 * n:512 * n + 512], ps[4 + n][:, 0:512],
                    x[:, 512 * n:512 * n + 512], ALU.add),
                    inc="VE" if nh == 1 else None)
            v_fin = em.cnt["VE"]
            bguardC[4] = ("VE", v_fin)
            bguardC[5] = ("VE", v_fin)
            em.wait("sync", "VE", v_fin)
            st_out[tt] = (f"STG{2 + slot}",
                          store(out_d[128 * tt:128 * tt + 128, :], sgap,
                                sem=f"STG{2 + slot}"))

        if run_c:
            emit_loads(0)
        for tt in range(NT if run_c else 0):
            slot = tt % 2
            ng_ap = ar[:, off["ngt"] + 1024 * slot:off["ngt"] + 1024 * slot + 1024]
            xc_ap = ar[:, off["xct"] + 1024 * slot:off["xct"] + 1024 * slot + 1024]
            x2_ap = ar[:, off["x2t"] + 1024 * slot:off["x2t"] + 1024 * slot + 1024]
            x2f_ap = ar[:, off["x2f"] + 1024 * slot:off["x2f"] + 1024 * slot + 1024]
            x2aT[tt] = x2_ap
            em.wait("tensor", f"LDX{slot}", ldxT[tt])
            if tt == 0:
                em.wait("tensor", "LD", ld_wo)
            for b in (0, 1):
                gd = bguardC[b]
                em.wait("tensor", gd[0], gd[1])
            for kc in range(8):
                for nh in range(2):
                    em.op("tensor", _mm(
                        ps[nh][:, 0:512], ng_ap[:, 128 * kc:128 * kc + 128],
                        ar[:, off["woD"] + 1024 * kc + 512 * nh:
                           off["woD"] + 1024 * kc + 512 * nh + 512],
                        kc == 0, kc == 7),
                        inc="PE" if (kc == 7 and nh == 1) else None)
            pe_wo = em.cnt["PE"]
            pe_woT[tt] = pe_wo
            if tt >= 1:
                stageR(tt - 1)
                lateC_mm(tt - 1)
            if tt + 1 < NT:
                emit_loads(tt + 1)
            em.wait("vector", "PE", pe_wo)
            em.wait("vector", f"LDX{slot}", ldxT[tt])
            for nh in range(2):
                em.op("vector", lambda e, n=nh, d=x2_ap, x=xc_ap: e.tensor_tensor(
                    d[:, 512 * n:512 * n + 512], ps[n][:, 0:512],
                    x[:, 512 * n:512 * n + 512], ALU.add))
            em.op("vector", lambda e, d=x2_ap: e.tensor_tensor(d, d, A("bo"), ALU.add),
                  inc="VE")
            v_x2 = em.cnt["VE"]
            v_x2T[tt] = v_x2
            bguardC[0] = ("VE", v_x2)
            bguardC[1] = ("VE", v_x2)
            stv = ar[:, off["lnstC"]:off["lnstC"] + 8]
            red_s, qsum, mu, m2, var, rinv, beta = (stv[:, i:i + 1] for i in range(7))
            sqC = ar[:, off["sqC"]:off["sqC"] + 1024]
            em.op("vector", lambda e, a=x2_ap, r=red_s: e.tensor_reduce(r, a, AX.X, ALU.add))
            em.wait("scalar", "VE", v_x2)
            em.op("scalar", lambda e, a=x2_ap, r=qsum, s=sqC: e.activation(
                s, a, AF.Square, accum_out=r), inc="SE")
            s_q2 = em.cnt["SE"]
            em.op("vector", lambda e, r=red_s, m=mu: e.tensor_scalar(m, r, 1.0 / D, None,
                                                                     ALU.mult))
            em.op("vector", lambda e, m=mu, o=m2: e.tensor_tensor(o, m, m, ALU.mult))
            em.wait("vector", "SE", s_q2)
            em.op("vector", lambda e, qq=qsum, o=var, m=m2: e.scalar_tensor_tensor(
                o, qq, 1.0 / D, m, ALU.mult, ALU.subtract), inc="VE")
            v_var = em.cnt["VE"]
            em.wait("scalar", "VE", v_var)
            em.op("scalar", lambda e, v_=var, r=rinv: e.activation(r, v_, AF.Ln, bias=LN_EPS))
            em.op("scalar", lambda e, r=rinv: e.activation(r, r, AF.Exp, scale=-0.5), inc="SE")
            s_rinv = em.cnt["SE"]
            em.wait("vector", "SE", s_rinv)
            em.op("vector", lambda e, m=mu, r=rinv, b=beta: e.tensor_scalar(
                b, m, r, -1.0, ALU.mult, ALU.mult), inc="VE")
            v_beta = em.cnt["VE"]
            em.wait("scalar", "VE", v_beta)
            em.op("scalar", lambda e, xs=x2_ap, xo=ng_ap, r=rinv, b=beta: e.activation(
                xo, xs, AF.Identity, scale=r, bias=b), inc="SE")
            s_xh = em.cnt["SE"]
            s_xhT[tt] = s_xh
            em.wait("tensor", "SE", s_xh)
            for b in (2, 3):
                gd = bguardC[b]
                em.wait("tensor", gd[0], gd[1])
            for fc in range(NFC):
                bank = 2 + fc // 4
                qq = fc % 4
                em.op("tensor", _tp(ps[bank][:, 128 * qq:128 * qq + 128],
                                    ng_ap[:, 128 * fc:128 * fc + 128], IDT,
                                    start=(qq == 0), stop=(qq == 3)),
                      inc="PE" if fc == NFC - 1 else None)
            pe_tp2 = em.cnt["PE"]
            pe_tp2T[tt] = pe_tp2
            em.wait("vector", "PE", pe_tp2)
            em.op("vector", lambda e, d=x2f_ap: e.tensor_copy(d[:, 0:512], ps[2][:, 0:512]))
            em.op("vector", lambda e, d=x2f_ap: e.tensor_copy(d[:, 512:1024], ps[3][:, 0:512]),
                  inc="VE")
            v_x2f = em.cnt["VE"]
            bguardC[2] = ("VE", v_x2f)
            bguardC[3] = ("VE", v_x2f)
            em.wait("tensor", "VE", v_x2f)
            gd = bguardC[6]
            em.wait("tensor", gd[0], gd[1])
            for kc in range(8):
                em.op("tensor", _mm(ps[6][:, 0:256], x2f_ap[:, 128 * kc:128 * kc + 128],
                                    ar[:, off["wpD"] + 256 * kc:off["wpD"] + 256 * kc + 256],
                                    kc == 0, kc == 7))
            for kc in range(8):
                em.op("tensor", _mm(ps[6][:, 256:260], x2f_ap[:, 128 * kc:128 * kc + 128],
                                    ar[:, off["simw"] + 4 * kc:off["simw"] + 4 * kc + 4],
                                    kc == 0, kc == 7),
                      inc="PE" if kc == 7 else None)
            pe_pl = em.cnt["PE"]
            gd = bguardC[7]
            em.wait("tensor", gd[0], gd[1])
            for kc in range(8):
                em.op("tensor", _mm(ps[7][0:15, 0:128],
                                    ar[:, off["wd"] + 15 * kc:off["wd"] + 15 * kc + 15],
                                    x2f_ap[:, 128 * kc:128 * kc + 128],
                                    kc == 0, kc == 7), inc="PE" if kc == 7 else None)
            pe_yw = em.cnt["PE"]
            em.wait("vector", "PE", pe_pl)
            proj_ap = ar[:, off["projt"]:off["projt"] + 256]
            em.op("vector", lambda e, d=proj_ap: e.tensor_tensor(d, ps[6][:, 0:256],
                                                                 A("bp"), ALU.add), inc="VE")
            v_proj = em.cnt["VE"]
            rC = ar[:, off["lnstC"] + 8:off["lnstC"] + 40]
            ssC = rC[:, 0:1]
            nm_ = rC[:, 1:2]
            sm_ = rC[:, 2:3]
            lg = rC[:, 4:8]
            exl = rC[:, 8:12]
            lw = rC[:, 12:16]
            msk = rC[:, 16:20]
            f_ = rC[:, 20:24]
            n012 = rC[:, 24:27]
            c01 = rC[:, 27:28]
            c012 = rC[:, 28:29]
            em.op("vector", lambda e, d=proj_ap, s=sqC: e.tensor_tensor(
                s[:, 0:256], d, d, ALU.mult))
            em.op("vector", lambda e, s=sqC, o=ssC: e.tensor_reduce(o, s[:, 0:256],
                                                                    AX.X, ALU.add))
            em.op("vector", lambda e, o=ssC: e.tensor_scalar(o, o, FFN_EPS2, None, ALU.max),
                  inc="VE")
            v_ssC = em.cnt["VE"]
            em.wait("scalar", "VE", v_ssC)
            em.op("scalar", lambda e, o=ssC: e.activation(o, o, AF.Ln))
            em.op("scalar", lambda e, o=ssC: e.activation(o, o, AF.Exp, scale=-0.5), inc="SE")
            s_rinvC = em.cnt["SE"]
            em.wait("vector", "SE", s_rinvC)
            em.op("vector", lambda e, o=lg: e.tensor_tensor(o, ps[6][:, 256:260], A("bpw"),
                                                            ALU.add), inc="VE")
            em.op("vector", lambda e, o=lg, s=ssC: e.tensor_scalar(
                o, o, s, ar[:, off["srep"]:off["srep"] + 1], ALU.mult, ALU.mult))
            em.op("vector", lambda e, o=lg, n=nm_: e.tensor_reduce(
                n, o, AX.X, ALU.max, negate=True), inc="VE")
            v_lg = em.cnt["VE"]
            em.wait("scalar", "VE", v_lg)
            em.op("scalar", lambda e, o=lg, x=exl, n=nm_: e.activation(
                x, o, AF.Exp, bias=n), inc="SE")
            s_ex = em.cnt["SE"]
            em.wait("vector", "SE", s_ex)
            em.op("vector", lambda e, x=exl, s=sm_: e.tensor_reduce(s, x, AX.X, ALU.add))
            em.op("vector", lambda e, s=sm_: e.reciprocal(s, s))
            em.op("vector", lambda e, x=exl, s=sm_, o=lw: e.tensor_scalar(
                o, x, s, None, ALU.mult))
            em.op("vector", lambda e, o=lw, n=nm_: e.tensor_reduce(
                n, o, AX.X, ALU.max, negate=True))
            em.op("vector", lambda e, o=lw, n=nm_, m=msk: e.tensor_scalar(
                m, o, n, 0.0, ALU.add, ALU.is_ge))
            em.op("vector", lambda e, m=msk, n=n012: e.tensor_scalar(
                n, m[:, 0:3], -1.0, 1.0, ALU.mult, ALU.add))
            em.op("vector", lambda e, n=n012, c=c01: e.tensor_tensor(
                c, n[:, 0:1], n[:, 1:2], ALU.mult))
            em.op("vector", lambda e, n=n012, c=c01, cc=c012: e.tensor_tensor(
                cc, c, n[:, 2:3], ALU.mult))
            em.op("vector", lambda e, m=msk, f=f_: e.tensor_copy(f[:, 0:1], m[:, 0:1]))
            em.op("vector", lambda e, m=msk, f=f_, n=n012: e.tensor_tensor(
                f[:, 1:2], n[:, 0:1], m[:, 1:2], ALU.mult))
            em.op("vector", lambda e, m=msk, f=f_, c=c01: e.tensor_tensor(
                f[:, 2:3], c, m[:, 2:3], ALU.mult))
            em.op("vector", lambda e, m=msk, f=f_, cc=c012: e.tensor_tensor(
                f[:, 3:4], cc, m[:, 3:4], ALU.mult))
            em.op("vector", lambda e, f=f_, o=lw: e.tensor_tensor(o, o, f, ALU.mult),
                  inc="VE")
            v_ret = em.cnt["VE"]
            rexp = ar[:, off["retf"]:off["retf"] + 15]
            em.op("vector", lambda e, o=lw, r=rexp: e.tensor_copy(r[:, 0:1], o[:, 0:1]))
            em.op("vector", lambda e, o=lw, r=rexp: e.tensor_copy(
                r[:, 1:3], o[:, 1:2].broadcast_to([128, 2])))
            em.op("vector", lambda e, o=lw, r=rexp: e.tensor_copy(
                r[:, 3:7], o[:, 2:3].broadcast_to([128, 4])))
            em.op("vector", lambda e, o=lw, r=rexp: e.tensor_copy(
                r[:, 7:15], o[:, 3:4].broadcast_to([128, 8])), inc="VE")
            v_rexpT[tt] = em.cnt["VE"]
            pe_ywT[tt] = pe_yw
            if tt >= 1:
                lateC_fin(tt - 1)
        if run_c:
            stageR(NT - 1)
            lateC_mm(NT - 1)
            lateC_fin(NT - 1)

        if not run_c:
            zb = ar[:, off["stgC"]:off["stgC"] + 1024]
            em.op("vector", lambda e, b=zb: e.memset(b, 0.0), inc="VE")
            vz = em.cnt["VE"]
            em.wait("sync", "VE", vz)
            for tt in range(NT):
                store(out_d[128 * tt:128 * tt + 128, :], zb, sem="STG2")
        wait_all_stores("sync")

        @block.tensor
        def _(e):
            for f in em.q["tensor"]:
                f(e)

        @block.vector
        def _(e):
            for f in em.q["vector"]:
                f(e)

        @block.scalar
        def _(e):
            for f in em.q["scalar"]:
                f(e)

        @block.gpsimd
        def _(e):
            for f in em.q["gpsimd"]:
                f(e)

        @block.sync
        def _(e):
            for f in em.q["sync"]:
                f(e)

    return nc


ng_store = {}

_NC_CACHE = {}


def _get_nc(debug=False):
    key = (bool(debug), PHASES, B_SC, B_STAGE)
    if key not in _NC_CACHE:
        ng_store.clear()
        _NC_CACHE[key] = build(debug)
    return _NC_CACHE[key]


def kernel(**inp):
    x = np.ascontiguousarray(np.asarray(inp["L0_attn_x"], dtype=np.float32))
    ins = {}
    for m, nm in zip("rwkvabg", ["W_r", "W_w", "W_k", "W_v", "W_a", "W_b", "W_g"]):
        ins[f"w_{m}T"] = np.ascontiguousarray(np.asarray(inp[f"L0_attn_{nm}"], np.float32).T)
    ins["wo_T"] = np.ascontiguousarray(np.asarray(inp["L0_attn_W_o"], np.float32).T)
    ins["ffnw_T"] = np.ascontiguousarray(np.asarray(inp["L0_ffn_W"], np.float32).T)
    ins["wp_T"] = np.ascontiguousarray(np.asarray(inp["L0_ffn_Wp"], np.float32).T)
    ins["wp_nat"] = np.ascontiguousarray(np.asarray(inp["L0_ffn_Wp"], np.float32))
    wd = np.concatenate([np.asarray(inp[f"L0_ffn_Wd{i}"], np.float32) for i in range(4)], 0)
    ins["wd_T"] = np.ascontiguousarray(wd.T)
    wu = np.concatenate([np.asarray(inp[f"L0_ffn_Wu{i}"], np.float32) for i in range(4)], 1)
    ins["wu_rho"] = np.ascontiguousarray(wu.T)
    ins["bo_rep"] = np.ascontiguousarray(
        np.broadcast_to(np.asarray(inp["L0_attn_b_o"], np.float32), (128, D)))
    ins["gnw_rep"] = np.ascontiguousarray(
        np.broadcast_to(np.asarray(inp["L0_attn_gn_w"], np.float32), (128, D)))
    ins["gnb_rep"] = np.ascontiguousarray(
        np.broadcast_to(np.asarray(inp["L0_attn_gn_b"], np.float32), (128, D)))
    ins["bp_rep"] = np.ascontiguousarray(
        np.broadcast_to(np.asarray(inp["L0_ffn_bp"], np.float32), (128, 256)))
    mixes = [np.asarray(inp[f"L0_attn_x_{m}"], np.float32).reshape(NFC, 128).T
             for m in "rwkvabg"]
    ins["mix_cat"] = np.ascontiguousarray(np.stack(mixes, 1).reshape(128, 56))
    ins["sim_in"] = np.ascontiguousarray(np.asarray(inp["L0_ffn_sim"], np.float32))
    ins["temp_in"] = np.ascontiguousarray(
        np.asarray(inp["L0_ffn_temp"], np.float32).reshape(1, 1))

    nc = _get_nc(DEBUG)
    in_maps = []
    for b in range(8):
        d = dict(ins)
        d["x_in"] = np.ascontiguousarray(x[b])
        in_maps.append(d)
    res = run_bass_kernel_spmd(nc, in_maps, core_ids=list(range(8)))
    out = np.stack([res.results[b]["out"] for b in range(8)], 0)
    if DEBUG:
        kernel._last = res
    return out


# version 27
# speedup vs baseline: 1.0095x; 1.0095x over previous
import contextlib
import numpy as np

import concourse.bass as bass
import concourse.mybir as mybir
from concourse.bass_utils import run_bass_kernel_spmd

F32 = mybir.dt.float32
AF = mybir.ActivationFunctionType
ALU = mybir.AluOpType
AX = mybir.AxisListType

T = 2048
D = 1024
H = 16
KH = 64
C = 64
SC = 256
NSC = T // SC
NT = T // 128
NFC = D // 128
GN_EPS = 0.00064
LN_EPS = 1e-5
NORM_EPS2 = 1e-12
FFN_EPS2 = 1e-24
CLAMP_MAX = float(np.log(1.0 / 0.01))

DEBUG = False
PHASES = 3
B_SC = 8
B_STAGE = 5

ENGS = ("tensor", "vector", "scalar", "sync", "gpsimd")
SEMS = ("LD", "LDW0", "LDW1", "LDX0", "LDX1", "LDB",
        "STG0", "STG1", "STG2", "STG3", "PE", "VE", "SE", "GE")
ENG2SEM = {"tensor": "PE", "vector": "VE", "scalar": "SE", "gpsimd": "GE"}


class Emitter:
    def __init__(self):
        self.q = {k: [] for k in ENGS}
        self.cnt = {k: 0 for k in SEMS}
        self.sems = {}

    def wait(self, eng, sem, val):
        if val > 0:
            self.q[eng].append(lambda e, s=sem, v=val: e.wait_ge(self.sems[s], v))

    def op(self, eng, fn, inc=None, n=1):
        if eng in ("vector", "scalar") and self.q[eng]:
            self.q[eng].append(lambda e: e.drain())
        if inc is None:
            self.q[eng].append(fn)
            return None
        self.cnt[inc] += n
        v = self.cnt[inc]
        if eng == "gpsimd":
            self.q[eng].append(fn)
            self.q[eng].append(lambda e: e.drain())
            self.q[eng].append(lambda e, s=inc, k=n: e.nop(nofuse=True).then_inc(self.sems[s], k))
        else:
            self.q[eng].append(lambda e, f=fn, s=inc, k=n: f(e).then_inc(self.sems[s], k))
        return v

    def mark(self, eng):
        sem = ENG2SEM[eng]
        self.cnt[sem] += 1
        v = self.cnt[sem]
        self.q[eng].append(lambda e: e.drain())
        self.q[eng].append(lambda e, s=sem: e.nop(nofuse=True).then_inc(self.sems[s], 1))
        return v

    def barrier(self):
        for eng in ("tensor", "vector", "scalar", "gpsimd"):
            self.mark(eng)
        snap = dict(self.cnt)
        for eng in ENGS:
            for sem in ("PE", "VE", "SE", "GE"):
                if ENG2SEM.get(eng) == sem:
                    continue
                self.wait(eng, sem, snap[sem])
        return snap


def _mm(ps_ap, lhsT, rhs, start, stop):
    def f(e):
        return e.matmul(ps_ap, lhsT, rhs, start=start, stop=stop, skip_group_check=True)
    return f


def _tp(ps_ap, in_ap, ident, start, stop):
    def f(e):
        return e.matmul(ps_ap, in_ap, ident, start=start, stop=stop,
                        is_transpose=True, skip_group_check=True)
    return f


def hbc(ap64, nblk=16):
    return ap64.unsqueeze(1).broadcast_to([128, nblk, 64])


def build(debug=False):
    nc = bass.Bass("TRN2", target_bir_lowering=False)
    em = Emitter()

    x_in = nc.dram_tensor("x_in", [T, D], F32, kind="ExternalInput").ap()
    wT = {m: nc.dram_tensor(f"w_{m}T", [D, D], F32, kind="ExternalInput").ap()
          for m in "rwkvabg"}
    wo_T = nc.dram_tensor("wo_T", [D, D], F32, kind="ExternalInput").ap()
    ffnw_T = nc.dram_tensor("ffnw_T", [D, D], F32, kind="ExternalInput").ap()
    wp_T = nc.dram_tensor("wp_T", [D, 256], F32, kind="ExternalInput").ap()
    wp_nat = nc.dram_tensor("wp_nat", [256, D], F32, kind="ExternalInput").ap()
    wd_T = nc.dram_tensor("wd_T", [D, 15], F32, kind="ExternalInput").ap()
    wu_rho = nc.dram_tensor("wu_rho", [15, D], F32, kind="ExternalInput").ap()
    bo_rep = nc.dram_tensor("bo_rep", [128, D], F32, kind="ExternalInput").ap()
    gnw_rep = nc.dram_tensor("gnw_rep", [128, D], F32, kind="ExternalInput").ap()
    gnb_rep = nc.dram_tensor("gnb_rep", [128, D], F32, kind="ExternalInput").ap()
    bp_rep = nc.dram_tensor("bp_rep", [128, 256], F32, kind="ExternalInput").ap()
    mix_cat = nc.dram_tensor("mix_cat", [128, 56], F32, kind="ExternalInput").ap()
    sim_in = nc.dram_tensor("sim_in", [256, 4], F32, kind="ExternalInput").ap()
    temp_in = nc.dram_tensor("temp_in", [1, 1], F32, kind="ExternalInput").ap()
    out_d = nc.dram_tensor("out", [T, D], F32, kind="ExternalOutput").ap()

    skind = "ExternalOutput" if debug else "Internal"
    scr_fm = {m: nc.dram_tensor(f"scr_{m}", [NFC, 128, T], F32, kind=skind).ap()
              for m in ("r", "p")}
    scr_tm = {m: nc.dram_tensor(f"scr_{m}", [T, D], F32, kind=skind).ap()
              for m in ("k", "v", "a", "b", "g")}
    scr_ng = nc.dram_tensor("scr_ng", [NFC, 128, T], F32, kind=skind).ap()
    dbg_o = nc.dram_tensor("dbg_o", [T, D], F32, kind=skind).ap() if debug else None
    dbg_sk = nc.dram_tensor("dbg_sk", [32, 128, 512], F32, kind=skind).ap() if debug else None
    dbg_kt = nc.dram_tensor("dbg_kt", [NSC, 128, 2048], F32, kind=skind).ap() if debug else None
    dbg_st = nc.dram_tensor("dbg_st", [32, 128, 512], F32, kind=skind).ap() if debug else None
    dbg_ps7 = nc.dram_tensor("dbg_ps7", [32, 128, 512], F32, kind=skind).ap() if debug else None
    dbg_skin = nc.dram_tensor("dbg_skin", [32, 128, 512], F32, kind=skind).ap() if debug else None
    dbg_bt = nc.dram_tensor("dbg_bt", [NSC, 128, 2048], F32, kind=skind).ap() if debug else None
    dbg_at = nc.dram_tensor("dbg_at", [NSC, 128, 2048], F32, kind=skind).ap() if debug else None
    dbg_zt = nc.dram_tensor("dbg_zt", [32, 128, 1024], F32, kind=skind).ap() if debug else None

    with contextlib.ExitStack() as stack:
        e_ = stack.enter_context
        AR = e_(nc.sbuf_tensor("arena", [128, 53120], F32))
        ar = AR.ap()
        PS = [e_(nc.psum_tensor(f"ps{i}", [128, 512], F32)) for i in range(8)]
        ps = [p.ap() for p in PS]
        for s in SEMS:
            em.sems[s] = e_(nc.semaphore(s))
        block = e_(nc.Block())

        off, _sz = {}, {}
        cur = [0]

        def AL(name, n):
            off[name] = cur[0]
            _sz[name] = n
            cur[0] += n
            return off[name]

        def A(name):
            return ar[:, off[name]:off[name] + _sz[name]]

        AL("ident", 128)
        AL("i64", 64)
        AL("maskU", 64)
        AL("maskL", 64)
        AL("maskUI", 64)
        AL("bdU", 128)
        AL("bdL", 128)
        AL("tiny", 16)
        AL("gnw", D)
        AL("gnb", D)
        AL("bo", D)
        AL("bp", 256)
        AL("mix", 56)
        AL("simn", 8)
        AL("simw", 32)
        AL("bpw", 4)
        AL("srep", 4)
        AL("wd", 120)
        AL("wu", 1024)
        AL("sk", 512)
        AL("sc0", 64)
        AL("cap", 8)
        PB = cur[0]
        assert 53120 - PB >= 44000, PB

        pcur = [PB]

        def pal(name, n):
            off[name] = pcur[0]
            _sz[name] = n
            pcur[0] += n
            assert pcur[0] <= 53120, (name, pcur[0])
            return off[name]

        def preset():
            pcur[0] = PB

        IDT = A("ident")
        lane = ar[:, off["tiny"]:off["tiny"] + 1]
        h64 = ar[:, off["tiny"] + 1:off["tiny"] + 2]
        lane64 = ar[:, off["tiny"] + 2:off["tiny"] + 3]

        def load(dst, src, sem="LD"):
            return em.op("sync", lambda e, d=dst, s=src: e.dma_start(out=d, in_=s),
                         inc=sem, n=16)

        def store(dst, src, sem="STG0"):
            return em.op("sync", lambda e, d=dst, s=src: e.dma_start(out=d, in_=s),
                         inc=sem, n=16)

        def wait_all_stores(eng):
            for sg in ("STG0", "STG1", "STG2", "STG3"):
                em.wait(eng, sg, em.cnt[sg])

        fidx128 = ar[:, PB:PB + 128]
        fidx64 = ar[:, off["sc0"]:off["sc0"] + 64]
        em.op("gpsimd", lambda e: e.iota(fidx128, pattern=[[1, 128]], base=0,
                                         channel_multiplier=0,
                                         allow_small_or_imprecise_dtypes=True))
        em.op("gpsimd", lambda e: e.iota(lane, pattern=[[0, 1]], base=0,
                                         channel_multiplier=1,
                                         allow_small_or_imprecise_dtypes=True), inc="GE")
        g0 = em.cnt["GE"]
        em.wait("vector", "GE", g0)
        em.op("vector", lambda e: e.tensor_scalar(A("ident"), fidx128, lane, None, ALU.is_equal))
        em.op("vector", lambda e: e.tensor_scalar(h64, lane, 64.0, None, ALU.is_ge))
        em.op("vector", lambda e: e.scalar_tensor_tensor(lane64, h64, -64.0, lane,
                                                         ALU.mult, ALU.add))
        em.op("vector", lambda e: e.tensor_copy(fidx64, fidx128[:, 0:64]))
        em.op("vector", lambda e: e.tensor_scalar(A("maskU"), fidx64, lane64, None, ALU.is_gt))
        em.op("vector", lambda e: e.tensor_scalar(A("maskL"), fidx64, lane64, None, ALU.is_lt))
        em.op("vector", lambda e: e.tensor_scalar(A("maskUI"), fidx64, lane64, None, ALU.is_ge))
        em.op("vector", lambda e: e.tensor_scalar(A("i64"), fidx64, lane64, None, ALU.is_equal))
        em.op("vector", lambda e: e.memset(A("bdU"), 0.0))
        em.op("vector", lambda e: e.memset(A("bdL"), 0.0))
        em.op("vector", lambda e: e.tensor_copy(A("bdU")[0:64, 0:64], A("maskU")[0:64, :]))
        em.op("vector", lambda e: e.tensor_copy(A("bdU")[64:128, 64:128], A("maskU")[64:128, :]))
        em.op("vector", lambda e: e.tensor_copy(A("bdL")[0:64, 0:64], A("maskL")[0:64, :]))
        em.op("vector", lambda e: e.tensor_copy(A("bdL")[64:128, 64:128], A("maskL")[64:128, :]))
        em.op("vector", lambda e: e.memset(A("sk"), 0.0))
        _cvals = [1.0, LN_EPS, GN_EPS, 1e-30]
        for _i, _v in enumerate(_cvals):
            cap_col = ar[:, off["cap"] + _i:off["cap"] + _i + 1]
            nc.const_aps.register(F32, _v, cap_col)
            em.op("vector", lambda e, c=cap_col, vv=_v: e.memset(c, vv))
        em.op("vector", lambda e: e.nop(), inc="VE")
        v_consts = em.cnt["VE"]

        load(A("gnw"), gnw_rep)
        load(A("gnb"), gnb_rep)
        load(A("bo"), bo_rep)
        load(A("bp"), bp_rep)
        load(A("mix"), mix_cat)
        load(ar[:, off["simn"]:off["simn"] + 4], sim_in[0:128, :])
        load(ar[:, off["simn"] + 4:off["simn"] + 8], sim_in[128:256, :])
        load(ar[0:1, off["srep"] + 2:off["srep"] + 3], temp_in)
        load(A("wd"), wd_T.rearrange("(c p) r -> p c r", p=128))
        load(ar[0:15, off["wu"]:off["wu"] + 1024], wu_rho)
        wpn0 = ar[:, PB + 256:PB + 1280]
        wpn1 = ar[:, PB + 1280:PB + 2304]
        load(wpn0, wp_nat[0:128, :])
        ld0 = load(wpn1, wp_nat[128:256, :])

        sq0 = ar[:, PB + 2304:PB + 2308]
        sq1 = ar[:, PB + 2308:PB + 2312]
        ones_strip = ar[0:1, PB + 2312:PB + 2440]
        em.wait("scalar", "LD", ld0)
        em.wait("scalar", "VE", v_consts)
        em.op("scalar", lambda e: e.activation(sq0, ar[:, off["simn"]:off["simn"] + 4], AF.Square))
        em.op("scalar", lambda e: e.activation(sq1, ar[:, off["simn"] + 4:off["simn"] + 8],
                                               AF.Square), inc="SE")
        s1 = em.cnt["SE"]
        em.wait("vector", "VE", v_consts)
        em.op("vector", lambda e: e.memset(ones_strip, 1.0), inc="VE")
        v1 = em.cnt["VE"]
        onescol = ar[:, PB + 2440:PB + 2441]
        em.op("vector", lambda e: e.memset(onescol, 1.0), inc="VE")
        v1b = em.cnt["VE"]
        em.wait("tensor", "SE", s1)
        em.wait("tensor", "VE", v1b)
        em.op("tensor", _mm(ps[0][0:1, 0:4], onescol, sq0, True, False))
        em.op("tensor", _mm(ps[0][0:1, 0:4], onescol, sq1, False, True), inc="PE")
        p1 = em.cnt["PE"]
        colinv1 = ar[0:1, PB + 2444:PB + 2448]
        em.wait("scalar", "PE", p1)
        em.op("scalar", lambda e: e.activation(colinv1, ps[0][0:1, 0:4], AF.Ln, bias=1e-30))
        em.op("scalar", lambda e: e.activation(colinv1, colinv1, AF.Exp, scale=-0.5), inc="SE")
        s2 = em.cnt["SE"]
        em.wait("tensor", "SE", s2)
        em.op("tensor", _mm(ps[1][:, 0:4], ones_strip, colinv1, True, True), inc="PE")
        p2 = em.cnt["PE"]
        colinv = ar[:, PB + 2448:PB + 2452]
        em.wait("vector", "PE", p2)
        em.op("vector", lambda e: e.tensor_copy(colinv, ps[1][:, 0:4]))
        em.op("vector", lambda e: e.tensor_tensor(ar[:, off["simn"]:off["simn"] + 4],
                                                  ar[:, off["simn"]:off["simn"] + 4],
                                                  colinv, ALU.mult))
        em.op("vector", lambda e: e.tensor_tensor(ar[:, off["simn"] + 4:off["simn"] + 8],
                                                  ar[:, off["simn"] + 4:off["simn"] + 8],
                                                  colinv, ALU.mult))
        em.op("vector", lambda e: e.tensor_scalar(ar[0:1, off["srep"] + 3:off["srep"] + 4],
                                                  ar[0:1, off["srep"] + 2:off["srep"] + 3],
                                                  CLAMP_MAX, None, ALU.min), inc="VE")
        v2 = em.cnt["VE"]
        em.wait("scalar", "VE", v2)
        em.op("scalar", lambda e: e.activation(ar[0:1, off["srep"] + 3:off["srep"] + 4],
                                               ar[0:1, off["srep"] + 3:off["srep"] + 4],
                                               AF.Exp), inc="SE")
        s3 = em.cnt["SE"]
        em.wait("tensor", "SE", s3)
        em.wait("tensor", "VE", v2)
        em.op("tensor", _mm(ps[1][:, 4:5], ones_strip,
                            ar[0:1, off["srep"] + 3:off["srep"] + 4], True, True), inc="PE")
        p3 = em.cnt["PE"]
        em.wait("vector", "PE", p3)
        em.op("vector", lambda e: e.tensor_copy(ar[:, off["srep"]:off["srep"] + 1],
                                                ps[1][:, 4:5]), inc="VE")

        for dc in range(8):
            em.op("tensor", _mm(ps[2][:, 4 * dc:4 * dc + 4], wpn0[:, 128 * dc:128 * dc + 128],
                                ar[:, off["simn"]:off["simn"] + 4], True, False))
            em.op("tensor", _mm(ps[2][:, 4 * dc:4 * dc + 4], wpn1[:, 128 * dc:128 * dc + 128],
                                ar[:, off["simn"] + 4:off["simn"] + 8], False, True),
                  inc="PE" if dc == 7 else None)
        p4 = em.cnt["PE"]
        em.wait("vector", "PE", p4)
        em.op("vector", lambda e: e.tensor_copy(A("simw"), ps[2][:, 0:32]), inc="VE")
        v3 = em.cnt["VE"]
        bpcol = ar[:, PB + 2452:PB + 2454]
        em.op("tensor", _tp(ps[3][:, 0:1], A("bp")[0:1, 0:128], IDT[0:1, 0:1], True, False))
        em.op("tensor", _tp(ps[3][:, 1:2], A("bp")[0:1, 128:256], IDT[0:1, 0:1], False, True), inc="PE")
        p5 = em.cnt["PE"]
        em.wait("vector", "PE", p5)
        em.op("vector", lambda e: e.tensor_copy(bpcol, ps[3][:, 0:2]), inc="VE")
        v4 = em.cnt["VE"]
        em.wait("tensor", "VE", v4)
        em.op("tensor", _mm(ps[3][0:4, 2:3], ar[:, off["simn"]:off["simn"] + 4],
                            bpcol[:, 0:1], True, False))
        em.op("tensor", _mm(ps[3][0:4, 2:3], ar[:, off["simn"] + 4:off["simn"] + 8],
                            bpcol[:, 1:2], False, True), inc="PE")
        p6 = em.cnt["PE"]
        bpw41 = ar[0:4, PB + 2454:PB + 2455]
        em.wait("vector", "PE", p6)
        em.op("vector", lambda e: e.tensor_copy(bpw41, ps[3][0:4, 2:3]), inc="VE")
        v5 = em.cnt["VE"]
        em.wait("tensor", "VE", v5)
        em.op("tensor", _tp(ps[3][0:1, 4:8], bpw41, IDT[0:4, 0:4], True, True), inc="PE")
        p7 = em.cnt["PE"]
        bpwrow = ar[0:1, PB + 2456:PB + 2460]
        em.wait("vector", "PE", p7)
        em.op("vector", lambda e: e.tensor_copy(bpwrow, ps[3][0:1, 4:8]), inc="VE")
        v6 = em.cnt["VE"]
        em.wait("tensor", "VE", v6)
        em.op("tensor", _mm(ps[3][:, 8:12], ones_strip, bpwrow, True, True), inc="PE")
        p8 = em.cnt["PE"]
        em.wait("vector", "PE", p8)
        em.op("vector", lambda e: e.tensor_copy(A("bpw"), ps[3][:, 8:12]), inc="VE")

        em.barrier()

        preset()
        XHW = 2049
        pal("xhf", NFC * XHW)
        pal("wcur", 8192)
        pal("wnxt", 8192)
        pal("stg", 4 * 512)
        pal("xt", 2 * 1024)
        pal("xh_t", 2 * 1024)
        pal("inp", 8 * 512)
        pal("dlt", 2 * 512)
        pal("lnst", 32)
        pal("sqscr", 1024)

        def xhf_ap(fc, c0, clen, lo=0, hi=128):
            base = off["xhf"] + fc * XHW + c0
            return ar[lo:hi, base:base + clen]

        xhf_all = ar[:, off["xhf"]:off["xhf"] + NFC * XHW]
        em.op("vector", lambda e: e.memset(
            xhf_all.rearrange("p (c w) -> p c w", w=XHW)[:, :, 0:1], 0.0), inc="VE")
        v_zcol = em.cnt["VE"]

        se_apply = {}
        ve_evac = {}
        pe_tpL = {}

        def ln1_tp(tt):
            slot = tt % 2
            xh_ap = ar[:, off["xh_t"] + 1024 * slot: off["xh_t"] + 1024 * slot + 1024]
            b0 = 4 + slot * 2
            em.wait("tensor", "SE", se_apply[tt])
            if tt >= 2:
                em.wait("tensor", "VE", ve_evac[tt - 2])
            for fc in range(NFC):
                bank = b0 + fc // 4
                q = fc % 4
                em.op("tensor", _tp(ps[bank][:, 128 * q:128 * q + 128],
                                    xh_ap[:, 128 * fc:128 * fc + 128], IDT,
                                    start=(q == 0), stop=(q == 3)),
                      inc="PE" if fc == NFC - 1 else None)
            pe_tp = em.cnt["PE"]
            pe_tpL[tt] = pe_tp
            em.wait("vector", "PE", pe_tp)
            for half in range(2):
                bank = b0 + half
                dst = xhf_all.rearrange("p (c w) -> p c w", w=XHW)[
                    :, 4 * half:4 * half + 4, 1 + 128 * tt:1 + 128 * tt + 128]
                em.op("vector", lambda e, b=bank, d=dst: e.tensor_copy(
                    d, ps[b][:, 0:512].rearrange("p (c w) -> p c w", w=128)),
                    inc="VE" if half == 1 else None)
            ve_evac[tt] = em.cnt["VE"]

        for tt in range(NT):
            slot = tt % 2
            xt_ap = ar[:, off["xt"] + 1024 * slot: off["xt"] + 1024 * slot + 1024]
            xh_ap = ar[:, off["xh_t"] + 1024 * slot: off["xh_t"] + 1024 * slot + 1024]
            if tt >= 2:
                em.wait("sync", "SE", se_apply[tt - 2])
                em.wait("sync", "PE", pe_tpL[tt - 2])
            ldx = load(xt_ap, x_in[128 * tt:128 * tt + 128, :], sem=f"LDX{slot}")
            em.wait("vector", f"LDX{slot}", ldx)
            em.wait("scalar", f"LDX{slot}", ldx)
            stv = ar[:, off["lnst"]:off["lnst"] + 8]
            red_s, qsum, mu, m2, var, rinv, beta = (stv[:, i:i + 1] for i in range(7))
            sqscr = ar[:, off["sqscr"]:off["sqscr"] + 1024]
            em.op("vector", lambda e, a=xt_ap, r=red_s: e.tensor_reduce(r, a, AX.X, ALU.add))
            em.op("scalar", lambda e, a=xt_ap, r=qsum, s=sqscr: e.activation(
                s, a, AF.Square, accum_out=r), inc="SE")
            s_q = em.cnt["SE"]
            em.op("vector", lambda e, r=red_s, m=mu: e.tensor_scalar(m, r, 1.0 / D, None,
                                                                     ALU.mult))
            em.op("vector", lambda e, m=mu, o=m2: e.tensor_tensor(o, m, m, ALU.mult))
            em.wait("vector", "SE", s_q)
            em.op("vector", lambda e, q=qsum, o=var, m=m2: e.scalar_tensor_tensor(
                o, q, 1.0 / D, m, ALU.mult, ALU.subtract), inc="VE")
            v_var = em.cnt["VE"]
            em.wait("scalar", "VE", v_var)
            em.op("scalar", lambda e, v_=var, r=rinv: e.activation(r, v_, AF.Ln, bias=LN_EPS))
            em.op("scalar", lambda e, r=rinv: e.activation(r, r, AF.Exp, scale=-0.5), inc="SE")
            s_rinv = em.cnt["SE"]
            em.wait("vector", "SE", s_rinv)
            em.op("vector", lambda e, m=mu, r=rinv, b=beta: e.tensor_scalar(
                b, m, r, -1.0, ALU.mult, ALU.mult), inc="VE")
            v_beta = em.cnt["VE"]
            em.wait("scalar", "VE", v_beta)
            se_apply[tt] = em.op("scalar", lambda e, xs=xt_ap, xo=xh_ap, r=rinv, b=beta:
                                 e.activation(xo, xs, AF.Identity, scale=r, bias=b), inc="SE")
            if tt >= 1:
                ln1_tp(tt - 1)
        ln1_tp(NT - 1)

        weights = ["r", "w", "k", "v", "a", "b", "g"]
        wslot_off = [off["wcur"], off["wnxt"]]
        ld_w = {}
        pe_wdone = {}

        def load_weight(mi):
            slot = mi % 2
            if mi >= 2:
                em.wait("sync", "PE", pe_wdone[mi - 2])
            v = None
            for kc in range(8):
                dst = ar[:, wslot_off[slot] + 1024 * kc: wslot_off[slot] + 1024 * kc + 1024]
                v = load(dst, wT[weights[mi]][128 * kc:128 * kc + 128, :], sem=f"LDW{slot}")
            ld_w[mi] = (f"LDW{slot}", v)

        load_weight(0)
        load_weight(1)

        stg_last = [None, None, None, None]
        stg_idx = [0]

        def stg_slot():
            i = stg_idx[0] % 4
            stg_idx[0] += 1
            return i, ar[:, off["stg"] + 512 * i: off["stg"] + 512 * i + 512]

        bank_guard = [("VE", 0)] * 8
        inp_prev_guard = [0] * 16
        for _j in range(8, 16):
            inp_prev_guard[_j] = pe_tpL[NT - 1]
        em.wait("vector", "SE", se_apply[NT - 1])
        em.wait("vector", "VE", v_zcol)
        em.wait("gpsimd", "VE", em.cnt["VE"])

        def inp_off(sl2):
            if sl2 < 8:
                return off["inp"] + 512 * sl2
            return off["xt"] + 512 * (sl2 - 8)

        jobs = [(mi2, t42) for mi2 in range(len(weights)) for t42 in range(4)]
        v_inp_j = {}

        def build_inp(j):
            mi2, t42 = jobs[j]
            t0b = 512 * t42
            sb = 8 * (j % 2)
            for kc in range(8):
                inp_ap = ar[:, inp_off(sb + kc): inp_off(sb + kc) + 512]
                dlt_ap = ar[:, off["dlt"]: off["dlt"] + 512]
                guard = inp_prev_guard[sb + kc]
                xh_sl = xhf_ap(kc, 1 + t0b, 512)
                xh_sh = xhf_ap(kc, t0b, 512)
                if guard:
                    em.wait("vector", "PE", guard)
                em.op("vector", lambda e, d2=dlt_ap, a=xh_sh, b=xh_sl: e.tensor_tensor(
                    d2, a, b, ALU.subtract))
                mixcol = ar[:, off["mix"] + 8 * mi2 + kc: off["mix"] + 8 * mi2 + kc + 1]
                em.op("vector", lambda e, i2=inp_ap, d2=dlt_ap, mc=mixcol, x=xh_sl:
                      e.scalar_tensor_tensor(i2, d2, mc, x, ALU.mult, ALU.add), inc="VE")
            v_inp_j[j] = em.cnt["VE"]

        build_inp(0)
        for j, (mi, t4) in enumerate(jobs):
            m = weights[mi]
            if t4 == 0:
                em.wait("tensor", ld_w[mi][0], ld_w[mi][1])
            wbase = wslot_off[mi % 2]
            fm = m in ("r", "w")
            if True:
                t0 = 512 * t4
                sbase = 8 * (j % 2)
                if j + 1 < len(jobs):
                    build_inp(j + 1)
                em.wait("tensor", "VE", v_inp_j[j])

                def inp_sl(kc, c0=0, clen=512):
                    return ar[:, inp_off(sbase + kc) + c0: inp_off(sbase + kc) + c0 + clen]

                if fm:
                    tgt = scr_fm["r" if m == "r" else "p"]
                    for ec in range(8):
                        gd = bank_guard[ec]
                        em.wait("tensor", gd[0], gd[1])
                        wcol = wbase + 128 * ec
                        for kc in range(8):
                            em.op("tensor", _mm(
                                ps[ec][:, 0:512],
                                ar[:, wcol + 1024 * kc: wcol + 1024 * kc + 128],
                                inp_sl(kc), kc == 0, kc == 7),
                                inc="PE" if kc == 7 else None)
                        pe_b = em.cnt["PE"]
                        si, stg_ap = stg_slot()
                        if stg_last[si]:
                            em.wait("scalar", stg_last[si][0], stg_last[si][1])
                        em.wait("scalar", "PE", pe_b)
                        if m == "w":
                            em.op("scalar", lambda e, b=ec, s2=stg_ap: e.activation(
                                s2, ps[b][:, 0:512], AF.Exp))
                            em.op("scalar", lambda e, s2=stg_ap: e.activation(
                                s2, s2, AF.Ln, bias=1.0), inc="SE")
                        else:
                            em.op("scalar", lambda e, b=ec, s2=stg_ap: e.copy(
                                s2, ps[b][:, 0:512]), inc="SE")
                        s_ev = em.cnt["SE"]
                        bank_guard[ec] = ("SE", s_ev)
                        em.wait("sync", "SE", s_ev)
                        stg_last[si] = (f"STG{si}",
                                        store(tgt[ec, :, t0:t0 + 512], stg_ap,
                                              sem=f"STG{si}"))
                else:
                    for sub in range(4):
                        si0, stg0 = stg_slot()
                        si1, stg1 = stg_slot()
                        assert si1 == si0 + 1
                        last = None
                        for nh, stg_ap, si in ((0, stg0, si0), (1, stg1, si1)):
                            b = sub * 2 + nh
                            gd = bank_guard[b]
                            em.wait("tensor", gd[0], gd[1])
                            for kc in range(8):
                                em.op("tensor", _mm(
                                    ps[b][:, 0:512], inp_sl(kc, 128 * sub, 128),
                                    ar[:, wbase + 1024 * kc + 512 * nh:
                                       wbase + 1024 * kc + 512 * nh + 512],
                                    kc == 0, kc == 7),
                                    inc="PE" if kc == 7 else None)
                            pe_b = em.cnt["PE"]
                            if stg_last[si]:
                                em.wait("scalar", stg_last[si][0], stg_last[si][1])
                                em.wait("vector", stg_last[si][0], stg_last[si][1])
                            em.wait("scalar", "PE", pe_b)
                            em.wait("vector", "PE", pe_b)
                            func = {"k": AF.Tanh, "g": AF.Sigmoid}.get(m)
                            if func is not None:
                                em.op("scalar", lambda e, bb=b, s2=stg_ap, f=func:
                                      e.activation(s2, ps[bb][:, 0:512], f), inc="SE")
                                bank_guard[b] = ("SE", em.cnt["SE"])
                                last = ("SE", em.cnt["SE"])
                            else:
                                em.op("vector", lambda e, bb=b, s2=stg_ap: e.tensor_copy(
                                    s2, ps[bb][:, 0:512]), inc="VE")
                                bank_guard[b] = ("VE", em.cnt["VE"])
                                last = ("VE", em.cnt["VE"])
                        full = ar[:, off["stg"] + 512 * si0: off["stg"] + 512 * si0 + 1024]
                        if m in ("a", "b"):
                            ss = ar[:, off["lnst"] + 8:off["lnst"] + 24]
                            sqscr = ar[:, off["sqscr"]:off["sqscr"] + 1024]
                            em.wait("vector", "SE", em.cnt["SE"])
                            em.op("vector", lambda e, f_=full, s2=sqscr: e.tensor_tensor(
                                s2, f_, f_, ALU.mult))
                            em.op("vector", lambda e, s2=ss, q2=sqscr: e.tensor_reduce(
                                s2, q2.rearrange("p (h k) -> p h k", k=64), AX.X, ALU.add))
                            em.op("vector", lambda e, s2=ss: e.tensor_scalar(
                                s2, s2, NORM_EPS2, None, ALU.max), inc="VE")
                            v_ss = em.cnt["VE"]
                            em.wait("scalar", "VE", v_ss)
                            em.op("scalar", lambda e, s2=ss: e.activation(s2, s2, AF.Ln))
                            em.op("scalar", lambda e, s2=ss: e.activation(
                                s2, s2, AF.Exp, scale=-0.5), inc="SE")
                            s_inv = em.cnt["SE"]
                            em.wait("vector", "SE", s_inv)
                            em.op("vector", lambda e, f_=full, s2=ss: e.tensor_tensor(
                                f_, f_, s2.unsqueeze(-1).broadcast_to([128, 16, 64]),
                                ALU.mult), inc="VE")
                            last = ("VE", em.cnt["VE"])
                        trow = 128 * (4 * t4 + sub)
                        em.wait("sync", last[0], last[1])
                        stv2 = (f"STG{si0}",
                                store(scr_tm[m][trow:trow + 128, :], full,
                                      sem=f"STG{si0}"))
                        stg_last[si0] = stv2
                        stg_last[si1] = stv2
                for _i in range(8):
                    inp_prev_guard[sbase + _i] = em.cnt["PE"]
            if t4 == 3:
                pe_wdone[mi] = em.cnt["PE"]
                if mi + 2 < len(weights):
                    load_weight(mi + 2)

        em.barrier()

        run_b = PHASES >= 2
        preset()
        pal("sl_r", 2048)
        pal("sl_p", 2048)
        pal("sl_k", 2048)
        pal("sl_v", 2048)
        pal("sl_a", 2048)
        pal("sl_b", 2048)
        pal("sl_g", 2048)
        pal("pc", 2048)
        pal("t2", 2048)
        pal("tinyB", 384)
        pal("pool", 25 * 1024)
        pal("stgB", 2 * 1024)

        pool_free = list(range(25))
        pool_map = {}

        def palloc(name):
            s = max(pool_free)
            pool_free.remove(s)
            pool_map[name] = s
            return ar[:, off["pool"] + 1024 * s: off["pool"] + 1024 * s + 1024]

        def pap(name):
            s = pool_map[name]
            if s < 0:
                s = -s - 1
                return ar[:, off["pool"] + 1024 * s: off["pool"] + 1024 * s + 2048]
            return ar[:, off["pool"] + 1024 * s: off["pool"] + 1024 * s + 1024]

        def bdalloc(name):
            for s in sorted(pool_free):
                if s + 1 in pool_free:
                    pool_free.remove(s)
                    pool_free.remove(s + 1)
                    pool_map[name] = -s - 1
                    return ar[:, off["pool"] + 1024 * s: off["pool"] + 1024 * s + 2048]
            raise AssertionError(("no adjacent free pair", sorted(pool_free)))

        def pfree(name):
            s = pool_map.pop(name)
            if s < 0:
                s = -s - 1
                pool_free.append(s)
                pool_free.append(s + 1)
            else:
                pool_free.append(s)

        evac_tog = [0]

        def evac_eng():
            evac_tog[0] = (evac_tog[0] + 1) % 3
            return "vector" if evac_tog[0] == 0 else "scalar"

        def ev_copy(dst, src, eng=None):
            eng = eng or evac_eng()
            if eng == "vector":
                v = em.op("vector", lambda e, d=dst, s=src: e.tensor_copy(d, s), inc="VE")
                return ("VE", v)
            v = em.op("scalar", lambda e, d=dst, s=src: e.copy(d, s), inc="SE")
            return ("SE", v)

        def ev_mask(dst, bank, mask64):
            v = em.op("vector", lambda e, d=dst, b=bank, mk=mask64: e.scalar_tensor_tensor(
                d.rearrange("p (a b) -> p a b", b=64),
                ps[b][:, 0:512].rearrange("p (a b) -> p a b", b=64),
                1.0, hbc(mk, 8), ALU.mult, ALU.mult), inc="VE")
            return ("VE", v)

        def tm_sl(name, q, h0, hl, p0=0, pl=128):
            base = off[name] + 1024 * q + 64 * h0
            return ar[p0:p0 + pl, base:base + 64 * hl]

        def rf_sl(h, q, pi):
            base = off["sl_r"] + 256 * (h // 2) + 128 * q + 64 * pi
            return ar[64 * (h % 2):64 * (h % 2) + 64, base:base + 64]

        tinyb = off["tinyB"]
        pm_ap = ar[:, tinyb:tinyb + 32]
        pe_ap2 = ar[:, tinyb + 32:tinyb + 64]
        pcs_ap = ar[:, tinyb + 64:tinyb + 96]
        ein_ap = ar[:, tinyb + 96:tinyb + 128]
        gc_ap = ar[:, tinyb + 128:tinyb + 160]
        gst = ar[:, tinyb + 160:tinyb + 192]

        pq_global = [0]
        bguard = [("VE", 0)] * 8
        wait_all_stores("sync")
        NSCR = min(NSC, B_SC) if run_b else 0
        tfs = {}

        def tf_loads(sci2, pe_guard):
            st = tfs.setdefault(sci2, {})
            t0b = SC * sci2
            if pe_guard is not None:
                em.wait("sync", "PE", pe_guard)
                em.wait("sync", "GE", tfs[sci2 - 1]["g_kb"])
            load(ar[:, off["sl_r"]:off["sl_r"] + 2048],
                 scr_fm["r"][:, :, t0b:t0b + SC].rearrange("c p t -> p c t"), sem="LDB")
            st["ldb_r"] = em.cnt["LDB"]
            for m in ("k", "v", "a", "b"):
                load(ar[:, off["sl_" + m]:off["sl_" + m] + 2048],
                     scr_tm[m][t0b:t0b + SC, :].rearrange("(n p) d -> p n d", p=128),
                     sem="LDB")
            st["ldb_ab"] = em.cnt["LDB"]

        def tf_T1(sci2):
            st = tfs[sci2]
            t0b = SC * sci2
            tb2 = off["tinyB"] + 192 * (sci2 % 2)
            pmA = ar[:, tb2:tb2 + 32]
            peA = ar[:, tb2 + 32:tb2 + 64]
            pcsA = ar[:, tb2 + 64:tb2 + 96]
            einA = ar[:, tb2 + 96:tb2 + 128]
            gcA = ar[:, tb2 + 128:tb2 + 160]
            prev = tfs.get(sci2 - 1)
            slp = ar[:, off["sl_p"]:off["sl_p"] + 2048]
            pc = ar[:, off["pc"]:off["pc"] + 2048]
            t2a = ar[:, off["t2"]:off["t2"] + 2048]
            if prev is not None:
                em.wait("sync", "PE", prev["pe_e2t"])
                em.wait("sync", "VE", prev["v_e2"])
            ldp = load(slp, scr_fm["p"][:, :, t0b:t0b + SC].rearrange("c p t -> p c t"),
                       sem="LDB")
            em.wait("vector", "LDB", ldp)
            if prev is not None:
                em.wait("vector", "PE", prev["pe_e2t"])
                em.wait("vector", "SE", prev["s_eA"])
            for fc in range(NFC):
                em.op("vector", lambda e, f=fc: e.tensor_tensor_scan(
                    pc[:, 256 * f:256 * f + 256], slp[:, 256 * f:256 * f + 256],
                    slp[:, 256 * f:256 * f + 256], 0.0, ALU.add, ALU.bypass),
                    inc="VE" if fc == NFC - 1 else None)
            pc3 = pc.rearrange("p (c s t) -> p c s t", s=4, t=64)
            em.op("vector", lambda e: e.tensor_copy(
                pmA.rearrange("p (c s) -> p c s", s=4),
                pc3[:, :, :, 31:32].squeeze(-1)))
            em.op("vector", lambda e: e.tensor_copy(
                peA.rearrange("p (c s) -> p c s", s=4),
                pc3[:, :, :, 63:64].squeeze(-1)))
            em.op("vector", lambda e: e.memset(pcsA, 0.0), inc="VE")
            em.op("vector", lambda e: e.tensor_copy(
                pcsA.rearrange("p (c s) -> p c s", s=4)[:, :, 1:4],
                peA.rearrange("p (c s) -> p c s", s=4)[:, :, 0:3]))
            em.op("vector", lambda e: e.tensor_tensor(einA, pcsA, pmA, ALU.subtract))
            em.op("vector", lambda e: e.tensor_tensor(gcA, pmA, peA, ALU.subtract),
                  inc="VE")
            v_eg = em.cnt["VE"]
            em.wait("scalar", "VE", v_eg)
            em.op("scalar", lambda e: e.activation(einA, einA, AF.Exp))
            em.op("scalar", lambda e: e.activation(gcA, gcA, AF.Exp), inc="SE")
            for s_ in range(4):
                pmv = pmA.rearrange("p (c s) -> p c s", s=4)[:, :, s_:s_ + 1].broadcast_to(
                    [128, 8, 64])
                pcv = pc.rearrange(
                    "p (c s t) -> p c s t", s=4, t=64)[:, :, s_:s_ + 1, :].squeeze(2)
                em.op("vector", lambda e, a=pcv, b=pmv: e.tensor_tensor(
                    a, a, b, ALU.subtract), inc="VE" if s_ == 3 else None)
            v_d1 = em.cnt["VE"]
            ea_p = bdalloc("Ea")
            em.wait("scalar", "VE", v_d1)
            if prev is not None:
                em.wait("scalar", "VE", prev["v_rhat"])
            em.op("scalar", lambda e, d=ea_p: e.activation(d, pc, AF.Exp, scale=-1.0),
                  inc="SE")
            st["s_eA"] = em.cnt["SE"]
            em.op("vector", lambda e: e.tensor_tensor(slp, pc, slp, ALU.subtract), inc="VE")
            st["v_e2"] = em.cnt["VE"]
            v_e2 = st["v_e2"]
            em.wait("tensor", "VE", v_d1)
            for q2 in range(2):
                for b in (0, 1):
                    gd = bguard[b]
                    em.wait("tensor", gd[0], gd[1])
                for fc in range(NFC):
                    bank = fc // 4
                    qq = fc % 4
                    src = ar[:, off["pc"] + 256 * fc + 128 * q2:
                             off["pc"] + 256 * fc + 128 * q2 + 128]
                    em.op("tensor", _tp(ps[bank][:, 128 * qq:128 * qq + 128], src, IDT,
                                        start=(qq == 0), stop=(qq == 3)),
                          inc="PE" if fc == NFC - 1 else None)
                pe_t = em.cnt["PE"]
                em.wait("vector", "PE", pe_t)
                if q2 == 0 and prev is not None:
                    em.wait("vector", "GE", prev["g_kb"])
                for half in range(2):
                    dst = ar[:, off["t2"] + 1024 * q2 + 512 * half:
                             off["t2"] + 1024 * q2 + 512 * half + 512]
                    em.op("vector", lambda e, d=dst, b=half: e.tensor_copy(
                        d, ps[b][:, 0:512]), inc="VE")
                bguard[0] = ("VE", em.cnt["VE"])
                bguard[1] = ("VE", em.cnt["VE"])
            v_d1t = em.cnt["VE"]
            em.wait("scalar", "VE", v_d1t)
            em.op("scalar", lambda e: e.activation(t2a, t2a, AF.Exp), inc="SE")
            st["s_ept"] = em.cnt["SE"]
            fa_p = bdalloc("Fa")
            em.wait("tensor", "VE", v_e2)
            for q2 in range(2):
                for b in (0, 1):
                    gd = bguard[b]
                    em.wait("tensor", gd[0], gd[1])
                for fc in range(NFC):
                    bank = fc // 4
                    qq = fc % 4
                    src = ar[:, off["sl_p"] + 256 * fc + 128 * q2:
                             off["sl_p"] + 256 * fc + 128 * q2 + 128]
                    em.op("tensor", _tp(ps[bank][:, 128 * qq:128 * qq + 128], src, IDT,
                                        start=(qq == 0), stop=(qq == 3)),
                          inc="PE" if fc == NFC - 1 else None)
                pe_t = em.cnt["PE"]
                em.wait("vector", "PE", pe_t)
                for half in range(2):
                    dst = fa_p[:, 1024 * q2 + 512 * half:1024 * q2 + 512 * half + 512]
                    em.op("vector", lambda e, d=dst, b=half: e.tensor_copy(
                        d, ps[b][:, 0:512]), inc="VE")
                bguard[0] = ("VE", em.cnt["VE"])
                bguard[1] = ("VE", em.cnt["VE"])
            st["pe_e2t"] = em.cnt["PE"]
            v_e2t = em.cnt["VE"]
            em.wait("scalar", "VE", v_e2t)
            em.op("scalar", lambda e, d=fa_p: e.activation(d, d, AF.Exp, scale=-1.0),
                  inc="SE")
            st["s_fat"] = em.cnt["SE"]
            st["ea"] = ea_p
            st["fa"] = fa_p

        def tf_gload(sci2):
            st = tfs[sci2]
            t0b = SC * sci2
            if sci2 > 0:
                em.wait("sync", "VE", tfs[sci2 - 1]["v_gate"])
            load(ar[:, off["sl_g"]:off["sl_g"] + 2048],
                 scr_tm["g"][t0b:t0b + SC, :].rearrange("(n p) d -> p n d", p=128),
                 sem="LDB")
            st["ldb_g"] = em.cnt["LDB"]

        def tf_T2(sci2):
            st = tfs[sci2]
            t2a = ar[:, off["t2"]:off["t2"] + 2048]
            em.wait("vector", "LDB", st["ldb_r"])
            em.wait("vector", "SE", st["s_eA"])
            slr = ar[:, off["sl_r"]:off["sl_r"] + 2048]
            em.op("vector", lambda e, a=st["ea"]: e.tensor_tensor(slr, slr, a, ALU.mult),
                  inc="VE")
            st["v_rhat"] = em.cnt["VE"]
            em.wait("gpsimd", "SE", st["s_ept"])
            em.wait("gpsimd", "LDB", st["ldb_ab"])
            slk = ar[:, off["sl_k"]:off["sl_k"] + 2048]
            slb = ar[:, off["sl_b"]:off["sl_b"] + 2048]
            em.op("gpsimd", lambda e: e.tensor_tensor(slk, slk, t2a, ALU.mult))
            em.op("gpsimd", lambda e: e.tensor_tensor(slb, slb, t2a, ALU.mult), inc="GE")
            st["g_kb"] = em.cnt["GE"]
            em.wait("vector", "SE", st["s_fat"])
            em.wait("vector", "LDB", st["ldb_ab"])
            sla = ar[:, off["sl_a"]:off["sl_a"] + 2048]
            em.op("vector", lambda e, f=st["fa"]: e.tensor_tensor(sla, sla, f, ALU.mult),
                  inc="VE")
            em.wait("vector", "GE", st["g_kb"])
            em.op("vector", lambda e: e.nop(nofuse=True), inc="VE")
            st["v_tf_done"] = em.cnt["VE"]
            pfree("Ea")
            pfree("Fa")

        for sci in range(NSCR):
            t0 = SC * sci
            par = sci % 2
            tinyb = off["tinyB"] + 192 * par
            pm_ap = ar[:, tinyb:tinyb + 32]
            pe_ap2 = ar[:, tinyb + 32:tinyb + 64]
            pcs_ap = ar[:, tinyb + 64:tinyb + 96]
            ein_ap = ar[:, tinyb + 96:tinyb + 128]
            gc_ap = ar[:, tinyb + 128:tinyb + 160]
            gst = ar[:, tinyb + 160:tinyb + 192]
            if sci == 0:
                tf_loads(0, None)
                tf_T1(0)
                tf_T2(0)
            tf_gload(sci)
            ldb_tot = tfs[sci]["ldb_ab"]
            v_tf_done = tfs[sci]["v_tf_done"]
            if debug:
                em.wait("sync", "VE", v_tf_done)
                for nmx, dbt in (("sl_k", dbg_kt), ("sl_b", dbg_bt), ("sl_a", dbg_at)):
                    dv2 = store(dbt[sci], ar[:, off[nmx]:off[nmx] + 2048], sem="STG3")
                    em.wait("sync", "STG3", dv2)
                em.wait("vector", "STG3", em.cnt["STG3"])

            ev_done = {}
            otT = {}
            def psl(name, h, pi):
                base = off["pool"] + 1024 * pool_map[name] + 64 * h
                return ar[64 * pi:64 * pi + 64, base:base + 64]

            def fpair(name, h):
                base = off["pool"] + 1024 * pool_map[name] + 128 * (h // 2)
                return ar[64 * (h % 2):64 * (h % 2) + 64, base:base + 128]

            def rpair(h, q2):
                base = off["sl_r"] + 256 * (h // 2) + 128 * q2
                return ar[64 * (h % 2):64 * (h % 2) + 64, base:base + 128]

            def bdb(name, h):
                s = pool_map[name]
                assert s < 0
                base = off["pool"] + 1024 * (-s - 1) + 128 * h
                return ar[:, base:base + 128]

            def bdc(name, h, pi):
                s = pool_map[name]
                base = off["pool"] + 1024 * (-s - 1) + 128 * h + 64 * pi
                return ar[64 * pi:64 * pi + 64, base:base + 64]

            def wait_ev(eng, *names):
                for nm in names:
                    for sem_, cnt_ in ev_done[nm]:
                        em.wait(eng, sem_, cnt_)

            def prod_call(b0, lname, rhs_fn):
                for b in range(b0, b0 + 4):
                    gd = bguard[b]
                    em.wait("tensor", gd[0], gd[1])
                for h in list(range(0, 16, 2)) + [None] + list(range(1, 16, 2)):
                    if h is None:
                        em.op("tensor", lambda e: e.drain())
                        continue
                    em.op("tensor", _mm(
                        ps[b0 + h // 4][:, 128 * (h % 4):128 * (h % 4) + 128],
                        fpair(lname, h), rhs_fn(h), True, True),
                        inc="PE" if h == 15 else None)
                return em.cnt["PE"]

            def ev_bd(name, b0, pe_v, mask_bd):
                em.wait("vector", "PE", pe_v)
                dstp = bdalloc(name)
                gs = []
                for i in range(4):
                    v = em.op("vector", lambda e, d=dstp[:, 512 * i:512 * i + 512],
                              b=b0 + i, mk=mask_bd: e.scalar_tensor_tensor(
                                  d.rearrange("p (a c) -> p a c", c=128),
                                  ps[b][:, 0:512].rearrange("p (a c) -> p a c", c=128),
                                  1.0, mk.unsqueeze(1).broadcast_to([128, 4, 128]),
                                  ALU.mult, ALU.mult), inc="VE")
                    gs.append(("VE", v))
                    bguard[b0 + i] = ("VE", v)
                ev_done[name] = gs
                return dstp

            def ev_diag(name, b0, pe_v, mask64):
                em.wait("vector", "PE", pe_v)
                dstp = palloc(name)
                gs = []
                for b in range(4):
                    for pi in range(2):
                        src = ps[b0 + b][64 * pi:64 * pi + 64, 0:512].rearrange(
                            "p (a two c) -> p a two c", two=2, c=64)[
                                :, :, pi:pi + 1, :].squeeze(2)
                        dst = dstp[64 * pi:64 * pi + 64,
                                   256 * b:256 * b + 256].rearrange(
                            "p (a c) -> p a c", c=64)
                        mk = mask64[64 * pi:64 * pi + 64, :]
                        v = em.op("vector", lambda e, d=dst, s=src, m=mk:
                                  e.scalar_tensor_tensor(
                                      d, s, 1.0,
                                      m.unsqueeze(1).broadcast_to([64, 4, 64]),
                                      ALU.mult, ALU.mult), inc="VE")
                    gs.append(("VE", v))
                    bguard[b0 + b] = ("VE", v)
                ev_done[name] = gs
                return dstp

            def mm_bd(b0, lname, rname):
                wait_ev("tensor", lname, rname)
                for b in range(b0, b0 + 4):
                    gd = bguard[b]
                    em.wait("tensor", gd[0], gd[1])
                for h in range(16):
                    em.op("tensor", _mm(
                        ps[b0 + h // 4][:, 128 * (h % 4):128 * (h % 4) + 128],
                        bdb(lname, h), bdb(rname, h), True, True),
                        inc="PE" if h == 15 else None)
                return em.cnt["PE"]

            def ev_bd_copy(name, b0, pe_v):
                em.wait("vector", "PE", pe_v)
                em.wait("scalar", "PE", pe_v)
                dstp = bdalloc(name)
                gs = []
                for i in range(4):
                    g = ev_copy(dstp[:, 512 * i:512 * i + 512], ps[b0 + i][:, 0:512])
                    gs.append(g)
                    bguard[b0 + i] = g
                ev_done[name] = gs
                return dstp

            def ev_bd_add(name, b0, pe_v, addname):
                em.wait("vector", "PE", pe_v)
                dstp = bdalloc(name)
                gs = []
                for i in range(4):
                    v = em.op("vector", lambda e, d=dstp[:, 512 * i:512 * i + 512],
                              b=b0 + i,
                              a_=pap(addname)[:, 512 * i:512 * i + 512]:
                              e.tensor_tensor(d, ps[b][:, 0:512], a_, ALU.add),
                              inc="VE")
                    gs.append(("VE", v))
                    bguard[b0 + i] = ("VE", v)
                ev_done[name] = gs
                return dstp

            bank_tog = [0]

            def nb():
                b0 = 4 * bank_tog[0]
                bank_tog[0] ^= 1
                return b0

            def phase1_gen(q):
                def N(n):
                    return n + str(q)

                fmg = {}
                for nm, src_name in (("afP", "sl_a"), ("bfP", "sl_b"), ("kfP", "sl_k")):
                    b0 = nb()
                    dstp = palloc(N(nm))
                    em.wait("tensor", "VE", v_tf_done if nm == "afP" else em.cnt["VE"])
                    for b in (b0 + 2, b0 + 3):
                        gd = bguard[b]
                        em.wait("tensor", gd[0], gd[1])
                    for hp in range(8):
                        bank = b0 + 2 + hp // 4
                        qq = hp % 4
                        src = tm_sl(src_name, q, 2 * hp, 2)
                        em.op("tensor", _tp(ps[bank][:, 128 * qq:128 * qq + 128], src, IDT,
                                            start=(qq == 0), stop=(qq == 3)),
                              inc="PE" if hp == 7 else None)
                    pe_t = em.cnt["PE"]
                    em.wait("vector", "PE", pe_t)
                    em.wait("scalar", "PE", pe_t)
                    g1 = ev_copy(dstp[:, 0:512], ps[b0 + 2][:, 0:512])
                    g2 = ev_copy(dstp[:, 512:1024], ps[b0 + 3][:, 0:512])
                    bguard[b0 + 2] = g1
                    bguard[b0 + 3] = g2
                    fmg[nm] = (g1, g2)
                    yield

                def wfm(*nms):
                    for nm2 in nms:
                        for g in fmg[nm2]:
                            em.wait("tensor", g[0], g[1])

                b0 = nb()
                wfm("afP", "bfP")
                p_ = prod_call(b0, N("bfP"), lambda h: fpair(N("afP"), h))
                ev_bd(N("Wu"), b0, p_, A("bdU"))
                yield
                b0 = nb()
                p_ = prod_call(b0, N("afP"), lambda h: fpair(N("bfP"), h))
                ev_bd(N("Wup"), b0, p_, A("bdL"))
                yield
                b0 = nb()
                wfm("kfP")
                p_ = prod_call(b0, N("kfP"), lambda h: fpair(N("afP"), h))
                ev_bd(N("Hu"), b0, p_, A("bdU"))
                yield
                b0 = nb()
                p_ = prod_call(b0, N("bfP"), lambda h: rpair(h, q))
                ev_diag(f"Qu{q}", b0, p_, A("maskUI"))
                yield
                b0 = nb()
                p_ = prod_call(b0, N("kfP"), lambda h: rpair(h, q))
                ev_diag(f"SA{q}", b0, p_, A("maskUI"))
                pfree(N("afP"))
                pfree(N("bfP"))
                pfree(N("kfP"))
                yield

                if B_STAGE < 3:
                    for nmf in list(pool_map):
                        pfree(nmf)
                    return

                b0 = nb()
                p_ = mm_bd(b0, N("Wup"), N("Wu"))
                ev_bd_copy(N("L2"), b0, p_)
                yield
                b0 = nb()
                pe_l2p = mm_bd(b0, N("Wu"), N("Wup"))
                ev_bd_copy(N("L2p"), b0, pe_l2p)
                em.wait("vector", "PE", pe_l2p)
                em.op("vector", lambda e, d=pap(N("Wup")): e.tensor_tensor(
                    d.rearrange("p (a c) -> p a c", c=128),
                    d.rearrange("p (a c) -> p a c", c=128),
                    IDT.unsqueeze(1).broadcast_to([128, 16, 128]), ALU.add), inc="VE")
                pool_map[N("M1p")] = pool_map.pop(N("Wup"))
                ev_done[N("M1p")] = [("VE", em.cnt["VE"])]
                pfree(N("Wu"))
                yield

                b0 = nb()
                hv_p = palloc(f"HV{q}")
                wait_ev("tensor", N("Hu"))
                em.wait("tensor", "LDB", ldb_tot)
                for b in (b0, b0 + 1):
                    gd = bguard[b]
                    em.wait("tensor", gd[0], gd[1])
                for h in range(16):
                    em.op("tensor", _mm(
                        ps[b0 + h // 8][:, 64 * (h % 8):64 * (h % 8) + 64],
                        bdb(N("Hu"), h), tm_sl("sl_v", q, h, 1), True, True),
                        inc="PE" if h == 15 else None)
                pe_hv = em.cnt["PE"]
                em.wait("vector", "PE", pe_hv)
                em.wait("scalar", "PE", pe_hv)
                g1 = ev_copy(hv_p[:, 0:512], ps[b0][:, 0:512])
                g2 = ev_copy(hv_p[:, 512:1024], ps[b0 + 1][:, 0:512])
                bguard[b0] = g1
                bguard[b0 + 1] = g2
                ev_done[f"HV{q}"] = [g1, g2]
                pfree(N("Hu"))
                yield

                chain = [("L4", "L2p", "L2", None), ("L4p", "L2", "L2p", ("L2p",)),
                         ("M2p", "L2", "M1p", ("L2", "M1p")),
                         ("L8", "L4p", "L4", None), ("L8p", "L4", "L4p", ("L4p",)),
                         ("M3p", "L4", "M2p", ("L4", "M2p")),
                         ("L16", "L8p", "L8", None), ("L16p", "L8", "L8p", ("L8p",)),
                         ("M4p", "L8", "M3p", ("L8", "M3p")),
                         ("L32", "L16p", "L16", ("L16p",)),
                         ("M5p", "L16", "M4p", ("L16", "M4p")),
                         ("M6p", "L32", "M5p", ("L32", "M5p"))]
                for out, ln_, rn_, frees in chain:
                    b0 = nb()
                    p_ = mm_bd(b0, N(ln_), N(rn_))
                    if out.startswith("M"):
                        ev_bd_add(N(out), b0, p_, N(rn_))
                    else:
                        ev_bd_copy(N(out), b0, p_)
                    if frees:
                        for f_ in frees:
                            pfree(N(f_))
                    yield
                b0 = nb()
                wait_ev("tensor", N("M6p"))
                for b in range(b0, b0 + 4):
                    gd = bguard[b]
                    em.wait("tensor", gd[0], gd[1])
                for h in range(16):
                    em.op("tensor", _tp(
                        ps[b0 + h // 4][:, 128 * (h % 4):128 * (h % 4) + 128],
                        bdb(N("M6p"), h), IDT, start=(h % 4 == 0), stop=(h % 4 == 3)),
                        inc="PE" if h == 15 else None)
                p_ = em.cnt["PE"]
                ev_bd_copy(f"M{q}", b0, p_)
                pfree(N("M6p"))

            gens = [phase1_gen(0), phase1_gen(1)] if B_STAGE >= 2 else []
            alive = [True] * len(gens)
            for _ in range(11 if gens else 0):
                try:
                    next(gens[0])
                except StopIteration:
                    alive[0] = False
            gi = 1
            while any(alive):
                if alive[gi]:
                    try:
                        next(gens[gi])
                    except StopIteration:
                        alive[gi] = False
                if len(gens) == 2:
                    gi = 1 - gi

            for q in range(2 if B_STAGE >= 4 else 0):
                ch0 = 2 * q
                hv_p = pap(f"HV{q}")
                cf_p = palloc("cf")
                zt_p = [palloc("Zt0"), palloc("Zt1")]
                skin_p = [palloc("Skin0"), palloc("Skin1")]
                stmp_p = palloc("stmp")

                def hv_sl(h, pi):
                    base = off["pool"] + 1024 * pool_map[f"HV{q}"] + 64 * h
                    return ar[64 * pi:64 * pi + 64, base:base + 64]

                def zt_sl(h, pi):
                    base = off["pool"] + 1024 * pool_map["Zt" + str(pi)] + 64 * h
                    return ar[64 * pi:64 * pi + 64, base:base + 64]

                def cf_sl(h, pi):
                    base = (off["pool"] + 1024 * pool_map["cf"] + 512 * pi
                            + 64 * (h // 2))
                    return ar[64 * (h % 2):64 * (h % 2) + 64, base:base + 64]

                def skin_sl(h, pi):
                    base = (off["pool"] + 1024 * pool_map["Skin" + str(pi)]
                            + 64 * (h // 2))
                    return ar[64 * (h % 2):64 * (h % 2) + 64, base:base + 64]

                pe_qstart = em.cnt["PE"]
                for pi in range(2):
                    gd = bguard[6]
                    em.wait("tensor", gd[0], gd[1])
                    wait_ev("tensor", f"M{q}")
                    for h in range(16):
                        em.op("tensor", _mm(
                            ps[6][64 * (h % 2):64 * (h % 2) + 64,
                                  64 * (h // 2):64 * (h // 2) + 64],
                            tm_sl("sl_a", q, h, 1, 64 * pi, 64), bdc(f"M{q}", h, pi),
                            (h // 2 == 0), (h // 2 == 7)),
                            inc="PE" if h == 15 else None)
                    pe_cf = em.cnt["PE"]
                    em.wait("vector", "PE", pe_cf)
                    gcf = ev_copy(cf_p[:, 512 * pi:512 * pi + 512], ps[6][:, 0:512],
                                  eng="vector")
                    bguard[6] = gcf
                gcf = gcf
                for pi in range(2):
                    ch = ch0 + pi
                    em.wait("vector", "PE", pe_qstart)
                    ein_b = ein_ap.rearrange("p (c s) -> p c s", s=4)[
                        :, :, ch:ch + 1].broadcast_to([128, 8, 64])
                    em.op("vector", lambda e, d=skin_p[pi], eb=ein_b: e.tensor_tensor(
                        d[:, 0:512].rearrange("p (a b) -> p a b", b=64),
                        A("sk").rearrange("p (a b) -> p a b", b=64), eb, ALU.mult), inc="VE")
                    v_skin = em.cnt["VE"]
                    if debug:
                        em.wait("sync", "VE", v_skin)
                        dvsk = store(dbg_skin[4 * sci + ch], skin_p[pi][:, 0:512],
                                     sem="STG3")
                        em.wait("vector", "STG3", dvsk)
                    for b in (4, 5):
                        gd = bguard[b]
                        em.wait("tensor", gd[0], gd[1])
                    em.wait("tensor", gcf[0], gcf[1])
                    for h in range(16):
                        b = 4 + h // 8
                        oap = ps[b][64 * pi:64 * pi + 64, 64 * (h % 8):64 * (h % 8) + 64]
                        em.op("tensor", _mm(oap, bdc(f"M{q}", h, pi), hv_sl(h, pi),
                                            (h % 8 == 0), False))
                    em.op("tensor", lambda e: e.drain())
                    em.wait("tensor", "VE", v_skin)
                    for h in list(range(0, 16, 2)) + [None] + list(range(1, 16, 2)):
                        if h is None:
                            em.op("tensor", lambda e: e.drain())
                            continue
                        b = 4 + h // 8
                        oap = ps[b][64 * pi:64 * pi + 64, 64 * (h % 8):64 * (h % 8) + 64]
                        em.op("tensor", _mm(oap, cf_sl(h, pi), skin_sl(h, pi),
                                            False, (h % 8 == 7)),
                              inc="PE" if h == 15 else None)
                    em.op("tensor", lambda e: e.drain())
                    pe_zt = em.cnt["PE"]
                    em.wait("vector", "PE", pe_zt)
                    em.wait("scalar", "PE", pe_zt)
                    g1 = ev_copy(zt_p[pi][64 * pi:64 * pi + 64, 0:512],
                                 ps[4][64 * pi:64 * pi + 64, 0:512])
                    g2 = ev_copy(zt_p[pi][64 * pi:64 * pi + 64, 512:1024],
                                 ps[5][64 * pi:64 * pi + 64, 0:512])
                    bguard[4] = g1
                    bguard[5] = g2
                    gd = bguard[7]
                    em.wait("tensor", gd[0], gd[1])
                    em.wait("tensor", g1[0], g1[1])
                    em.wait("tensor", g2[0], g2[1])
                    for h in range(16):
                        oap = ps[7][64 * (h % 2):64 * (h % 2) + 64,
                                    64 * (h // 2):64 * (h // 2) + 64]
                        em.op("tensor", _mm(oap, tm_sl("sl_b", q, h, 1, 64 * pi, 64),
                                            zt_sl(h, pi), (h // 2 == 0), False))
                        em.op("tensor", _mm(oap, tm_sl("sl_k", q, h, 1, 64 * pi, 64),
                                            tm_sl("sl_v", q, h, 1, 64 * pi, 64),
                                            False, (h // 2 == 7)),
                              inc="PE" if h == 15 else None)
                    pe_sn = em.cnt["PE"]
                    if debug:
                        em.wait("vector", "PE", pe_sn)
                        bq = ar[:, off["t2"]:off["t2"] + 512]
                        em.op("vector", lambda e, b=bq: e.tensor_copy(b, ps[7][:, 0:512]),
                              inc="VE")
                        v_d7 = em.cnt["VE"]
                        em.wait("sync", "VE", v_d7)
                        dv7 = store(dbg_ps7[4 * sci + ch], bq, sem="STG3")
                        em.wait("vector", "STG3", dv7)
                    gc_b = gc_ap.rearrange("p (c s) -> p c s", s=4)[
                        :, :, ch:ch + 1].broadcast_to([128, 8, 64])
                    em.wait("vector", "PE", pe_sn)
                    em.op("vector", lambda e, s_=stmp_p, sp=skin_p[pi]: e.tensor_tensor(
                        s_[:, 0:512], ps[7][:, 0:512], sp[:, 0:512], ALU.add))
                    em.op("vector", lambda e, gb=gc_b, s_=stmp_p: e.tensor_tensor(
                        A("sk").rearrange("p (a b) -> p a b", b=64),
                        s_[:, 0:512].rearrange("p (a b) -> p a b", b=64), gb, ALU.mult),
                        inc="VE")
                    v_sk = em.cnt["VE"]
                    bguard[7] = ("VE", v_sk)
                    if debug:
                        em.wait("sync", "VE", v_sk)
                        dvs = store(dbg_st[4 * sci + ch], stmp_p[:, 0:512], sem="STG3")
                        em.wait("sync", "STG3", dvs)
                        dvk = store(dbg_sk[4 * sci + ch], A("sk"), sem="STG3")
                        em.wait("sync", "STG3", dvk)
                        dvz = store(dbg_zt[4 * sci + ch], zt_p[pi], sem="STG3")
                        em.wait("vector", "STG3", dvz)
                    if pi == 0:
                        for b in (2, 3):
                            gd = bguard[b]
                            em.wait("tensor", gd[0], gd[1])
                    first = set()
                    for h in list(range(0, 16, 2)) + [None] + list(range(1, 16, 2)):
                        if h is None:
                            em.op("tensor", lambda e: e.drain())
                            continue
                        b = 2 + h // 8
                        oap = ps[b][64 * pi:64 * pi + 64, 64 * (h % 8):64 * (h % 8) + 64]
                        st_ = b not in first
                        first.add(b)
                        em.op("tensor", _mm(oap, rf_sl(h, q, pi), skin_sl(h, pi),
                                            st_, False))
                    em.op("tensor", lambda e: e.drain())
                    for h in range(16):
                        b = 2 + h // 8
                        oap = ps[b][64 * pi:64 * pi + 64, 64 * (h % 8):64 * (h % 8) + 64]
                        em.op("tensor", _mm(oap, psl(f"Qu{q}", h, pi), zt_sl(h, pi),
                                            False, False))
                        em.op("tensor", _mm(oap, psl(f"SA{q}", h, pi),
                                            tm_sl("sl_v", q, h, 1, 64 * pi, 64),
                                            False, (h % 8 == 7)),
                              inc="PE" if h == 15 else None)
                pe_o = em.cnt["PE"]
                pfree(f"HV{q}")
                pfree("cf")
                pfree("Zt0")
                pfree("Zt1")
                pfree("Skin0")
                pfree("Skin1")
                pfree("stmp")
                pfree(f"Qu{q}")
                pfree(f"SA{q}")
                pfree(f"M{q}")

                if B_STAGE < 5:
                    continue
                ot_p = palloc(f"Ot{q}")
                sq_p = palloc(f"sqB{q}")
                otT[q] = (ot_p, sq_p)
                em.wait("vector", "PE", pe_o)
                em.op("vector", lambda e, d=ot_p: e.tensor_copy(d[:, 0:512], ps[2][:, 0:512]))
                em.op("vector", lambda e, d=ot_p: e.tensor_copy(d[:, 512:1024], ps[3][:, 0:512]),
                      inc="VE")
                v_ot = em.cnt["VE"]
                bguard[2] = ("VE", v_ot)
                bguard[3] = ("VE", v_ot)
                if debug:
                    em.wait("sync", "VE", v_ot)
                    trow_d = t0 + 128 * q
                    dv = store(dbg_o[trow_d:trow_d + 128, :], ot_p, sem="STG3")
                    em.wait("vector", "STG3", dv)
                if q == 1 and sci + 1 < NSCR:
                    tf_loads(sci + 1, pe_o)
                    tf_T1(sci + 1)
                    tf_T2(sci + 1)

            for q in range(2 if B_STAGE >= 5 else 0):
                ot_p, sq_p = otT[q]
                mus = gst[:, 0:16]
                qs = ar[:, tinyb + 160 + 16:tinyb + 160 + 32]
                em.op("vector", lambda e, d=ot_p, s=sq_p: e.tensor_tensor(s, d, d, ALU.mult))
                em.op("vector", lambda e, d=ot_p, m_=mus: e.tensor_reduce(
                    m_, d.rearrange("p (h k) -> p h k", k=64), AX.X, ALU.add))
                em.op("vector", lambda e, s=sq_p, qq=qs: e.tensor_reduce(
                    qq, s.rearrange("p (h k) -> p h k", k=64), AX.X, ALU.add))
                em.op("vector", lambda e, m_=mus: e.tensor_scalar(m_, m_, 1.0 / KH, None,
                                                                  ALU.mult))
                em.op("vector", lambda e, s=sq_p, m_=mus: e.tensor_tensor(
                    s[:, 0:16], m_, m_, ALU.mult))
                em.op("vector", lambda e, qq=qs, s=sq_p: e.scalar_tensor_tensor(
                    qq, qq, 1.0 / KH, s[:, 0:16], ALU.mult, ALU.subtract), inc="VE")
                v_gn1 = em.cnt["VE"]
                em.wait("scalar", "VE", v_gn1)
                em.op("scalar", lambda e, qq=qs: e.activation(qq, qq, AF.Ln, bias=GN_EPS))
                em.op("scalar", lambda e, qq=qs: e.activation(qq, qq, AF.Exp, scale=-0.5),
                      inc="SE")
                s_rstd = em.cnt["SE"]
                em.wait("vector", "SE", s_rstd)
                em.op("vector", lambda e, m_=mus, qq=qs: e.scalar_tensor_tensor(
                    m_, m_, -1.0, qq, ALU.mult, ALU.mult))
                em.op("vector", lambda e, d=ot_p, qq=qs: e.tensor_tensor(
                    d.rearrange("p (h k) -> p h k", k=64),
                    d.rearrange("p (h k) -> p h k", k=64),
                    qq.unsqueeze(-1).broadcast_to([128, 16, 64]), ALU.mult))
                em.op("vector", lambda e, d=ot_p, m_=mus: e.tensor_tensor(
                    d.rearrange("p (h k) -> p h k", k=64),
                    d.rearrange("p (h k) -> p h k", k=64),
                    m_.unsqueeze(-1).broadcast_to([128, 16, 64]), ALU.add))
                em.op("vector", lambda e, d=ot_p: e.tensor_tensor(d, d, A("gnw"), ALU.mult))
                em.op("vector", lambda e, d=ot_p: e.tensor_tensor(d, d, A("gnb"), ALU.add))
                gsl = ar[:, off["sl_g"] + 1024 * q:off["sl_g"] + 1024 * q + 1024]
                em.wait("vector", "LDB", tfs[sci]["ldb_g"])
                em.op("vector", lambda e, d=ot_p, g_=gsl: e.tensor_tensor(
                    d, d, g_, ALU.mult), inc="VE")
                v_ng = em.cnt["VE"]
                if q == 1:
                    tfs[sci]["v_gate"] = v_ng
                em.wait("tensor", "VE", v_ng)
                for b in (0, 1):
                    gd = bguard[b]
                    em.wait("tensor", gd[0], gd[1])
                for fc in range(NFC):
                    bank = fc // 4
                    qq = fc % 4
                    em.op("tensor", _tp(ps[bank][:, 128 * qq:128 * qq + 128],
                                        ot_p[:, 128 * fc:128 * fc + 128], IDT,
                                        start=(qq == 0), stop=(qq == 3)),
                          inc="PE" if fc == NFC - 1 else None)
                pe_ngt = em.cnt["PE"]
                em.wait("vector", "PE", pe_ngt)
                sgB = ar[:, off["stgB"]:off["stgB"] + 2048]
                pqg = pq_global[0]
                sslot = pqg % 2
                sgap = ar[:, off["stgB"] + 1024 * sslot:off["stgB"] + 1024 * sslot + 1024]
                if pqg >= 2:
                    em.wait("vector", f"STG{sslot}", ng_store[pqg - 2])
                em.op("vector", lambda e, s=sgap: e.tensor_copy(s[:, 0:512], ps[0][:, 0:512]))
                em.op("vector", lambda e, s=sgap: e.tensor_copy(s[:, 512:1024], ps[1][:, 0:512]),
                      inc="VE")
                v_ngs = em.cnt["VE"]
                bguard[0] = ("VE", v_ngs)
                bguard[1] = ("VE", v_ngs)
                em.wait("sync", "VE", v_ngs)
                trow = t0 + 128 * q
                ng_store[pqg] = store(
                    scr_ng[:, :, trow:trow + 128].rearrange("c p t -> p c t"),
                    sgap.rearrange("p (c t) -> p c t", t=128), sem=f"STG{sslot}")
                pq_global[0] += 1
                pfree(f"Ot{q}")
                pfree(f"sqB{q}")
            if B_STAGE >= 5:
                assert len(pool_free) == 25, (len(pool_free), pool_map)

        em.barrier()

        run_c = PHASES >= 3
        preset()
        pal("woD", 8192)
        pal("fwD", 8192)
        pal("wpD", 2048)
        pal("ngt", 2 * 1024)
        pal("xct", 2 * 1024)
        pal("x2t", 2 * 1024)
        pal("x2f", 2 * 1024)
        pal("projt", 512)
        pal("retf", 1024)
        pal("stgC", 2 * 1024)
        pal("lnstC", 64)
        pal("sqC", 1024)

        wait_all_stores("sync")
        ldv = None
        for kc in range(8):
            ldv = load(ar[:, off["woD"] + 1024 * kc:off["woD"] + 1024 * kc + 1024],
                       wo_T[128 * kc:128 * kc + 128, :])
        load(ar[:, off["wpD"]:off["wpD"] + 2048],
             wp_T.rearrange("(c p) r -> p c r", p=128))
        ld_wo = em.cnt["LD"]
        for kc in range(8):
            ldv = load(ar[:, off["fwD"] + 1024 * kc:off["fwD"] + 1024 * kc + 1024],
                       ffnw_T[128 * kc:128 * kc + 128, :])
        ld_cw = em.cnt["LD"]

        bguardC = [("VE", 0)] * 8
        st_out = {}
        pe_woT = {}
        v_x2T = {}
        v_ywgT = {}
        ldxT = {}
        x2aT = {}
        ywgT = {}

        def emit_loads(tt):
            slot = tt % 2
            ng_ap = ar[:, off["ngt"] + 1024 * slot:off["ngt"] + 1024 * slot + 1024]
            xc_ap = ar[:, off["xct"] + 1024 * slot:off["xct"] + 1024 * slot + 1024]
            if tt >= 2:
                em.wait("sync", "PE", pe_tp2T[tt - 2])
                em.wait("sync", "SE", s_xhT[tt - 2])
                em.wait("sync", "VE", v_x2T[tt - 2])
            load(ng_ap.rearrange("p (c t) -> p c t", t=128),
                 scr_ng[:, :, 128 * tt:128 * tt + 128].rearrange("c p t -> p c t"),
                 sem=f"LDX{slot}")
            ldxT[tt] = load(xc_ap, x_in[128 * tt:128 * tt + 128, :], sem=f"LDX{slot}")

        pe_ffnT = {}
        v_rexpT = {}
        pe_ywT = {}
        pe_tp2T = {}
        s_xhT = {}

        def stageR(tt):
            slot = tt % 2
            rexp = ar[:, off["retf"]:off["retf"] + 15]
            em.wait("tensor", "VE", v_rexpT[tt])
            gd = bguardC[6]
            em.wait("tensor", gd[0], gd[1])
            em.op("tensor", _tp(ps[6][0:15, 384:512], rexp, IDT, True, True), inc="PE")
            pe_rt = em.cnt["PE"]
            em.wait("vector", "PE", pe_rt)
            retf = ar[0:15, off["retf"] + 128:off["retf"] + 256]
            em.op("vector", lambda e, r=retf: e.tensor_copy(r, ps[6][0:15, 384:512]),
                  inc="VE")
            v_retf = em.cnt["VE"]
            bguardC[6] = ("VE", v_retf)
            ywg = ar[0:15, off["retf"] + 256 + 128 * slot:off["retf"] + 384 + 128 * slot]
            ywgT[tt] = ywg
            em.wait("vector", "PE", pe_ywT[tt])
            em.op("vector", lambda e, y=ywg, r=retf: e.tensor_tensor(
                y, ps[7][0:15, 0:128], r, ALU.mult), inc="VE")
            v_ywg = em.cnt["VE"]
            v_ywgT[tt] = v_ywg
            bguardC[7] = ("VE", v_ywg)

        def lateC_mm(tt):
            slot = tt % 2
            x2f_ap = ar[:, off["x2f"] + 1024 * slot:off["x2f"] + 1024 * slot + 1024]
            if tt == 0:
                em.wait("tensor", "LD", ld_cw)
            em.wait("tensor", "VE", v_ywgT[tt])
            for b in (4, 5):
                gd = bguardC[b]
                em.wait("tensor", gd[0], gd[1])
            for kc in range(8):
                for nh in range(2):
                    em.op("tensor", _mm(
                        ps[4 + nh][:, 0:512], x2f_ap[:, 128 * kc:128 * kc + 128],
                        ar[:, off["fwD"] + 1024 * kc + 512 * nh:
                           off["fwD"] + 1024 * kc + 512 * nh + 512],
                        kc == 0, False))
            for nh in range(2):
                em.op("tensor", _mm(ps[4 + nh][:, 0:512], ywgT[tt],
                                    ar[0:15, off["wu"] + 512 * nh:off["wu"] + 512 * nh + 512],
                                    False, True), inc="PE" if nh == 1 else None)
            pe_ffnT[tt] = em.cnt["PE"]

        def lateC_fin(tt):
            slot = tt % 2
            x2_ap = x2aT[tt]
            em.wait("vector", "PE", pe_ffnT[tt])
            if tt >= 2:
                em.wait("vector", st_out[tt - 2][0], st_out[tt - 2][1])
            sgap = ar[:, off["stgC"] + 1024 * slot:off["stgC"] + 1024 * slot + 1024]
            for nh in range(2):
                em.op("vector", lambda e, n=nh, s=sgap, x=x2_ap: e.tensor_tensor(
                    s[:, 512 * n:512 * n + 512], ps[4 + n][:, 0:512],
                    x[:, 512 * n:512 * n + 512], ALU.add),
                    inc="VE" if nh == 1 else None)
            v_fin = em.cnt["VE"]
            bguardC[4] = ("VE", v_fin)
            bguardC[5] = ("VE", v_fin)
            em.wait("sync", "VE", v_fin)
            st_out[tt] = (f"STG{2 + slot}",
                          store(out_d[128 * tt:128 * tt + 128, :], sgap,
                                sem=f"STG{2 + slot}"))

        if run_c:
            emit_loads(0)
        for tt in range(NT if run_c else 0):
            slot = tt % 2
            ng_ap = ar[:, off["ngt"] + 1024 * slot:off["ngt"] + 1024 * slot + 1024]
            xc_ap = ar[:, off["xct"] + 1024 * slot:off["xct"] + 1024 * slot + 1024]
            x2_ap = ar[:, off["x2t"] + 1024 * slot:off["x2t"] + 1024 * slot + 1024]
            x2f_ap = ar[:, off["x2f"] + 1024 * slot:off["x2f"] + 1024 * slot + 1024]
            x2aT[tt] = x2_ap
            em.wait("tensor", f"LDX{slot}", ldxT[tt])
            if tt == 0:
                em.wait("tensor", "LD", ld_wo)
            for b in (0, 1):
                gd = bguardC[b]
                em.wait("tensor", gd[0], gd[1])
            for kc in range(8):
                for nh in range(2):
                    em.op("tensor", _mm(
                        ps[nh][:, 0:512], ng_ap[:, 128 * kc:128 * kc + 128],
                        ar[:, off["woD"] + 1024 * kc + 512 * nh:
                           off["woD"] + 1024 * kc + 512 * nh + 512],
                        kc == 0, kc == 7),
                        inc="PE" if (kc == 7 and nh == 1) else None)
            pe_wo = em.cnt["PE"]
            pe_woT[tt] = pe_wo
            if tt >= 1:
                stageR(tt - 1)
                lateC_mm(tt - 1)
            if tt + 1 < NT:
                emit_loads(tt + 1)
            em.wait("vector", "PE", pe_wo)
            em.wait("vector", f"LDX{slot}", ldxT[tt])
            for nh in range(2):
                em.op("vector", lambda e, n=nh, d=x2_ap, x=xc_ap: e.tensor_tensor(
                    d[:, 512 * n:512 * n + 512], ps[n][:, 0:512],
                    x[:, 512 * n:512 * n + 512], ALU.add))
            em.op("vector", lambda e, d=x2_ap: e.tensor_tensor(d, d, A("bo"), ALU.add),
                  inc="VE")
            v_x2 = em.cnt["VE"]
            v_x2T[tt] = v_x2
            bguardC[0] = ("VE", v_x2)
            bguardC[1] = ("VE", v_x2)
            stv = ar[:, off["lnstC"]:off["lnstC"] + 8]
            red_s, qsum, mu, m2, var, rinv, beta = (stv[:, i:i + 1] for i in range(7))
            sqC = ar[:, off["sqC"]:off["sqC"] + 1024]
            em.op("vector", lambda e, a=x2_ap, r=red_s: e.tensor_reduce(r, a, AX.X, ALU.add))
            em.wait("scalar", "VE", v_x2)
            em.op("scalar", lambda e, a=x2_ap, r=qsum, s=sqC: e.activation(
                s, a, AF.Square, accum_out=r), inc="SE")
            s_q2 = em.cnt["SE"]
            em.op("vector", lambda e, r=red_s, m=mu: e.tensor_scalar(m, r, 1.0 / D, None,
                                                                     ALU.mult))
            em.op("vector", lambda e, m=mu, o=m2: e.tensor_tensor(o, m, m, ALU.mult))
            em.wait("vector", "SE", s_q2)
            em.op("vector", lambda e, qq=qsum, o=var, m=m2: e.scalar_tensor_tensor(
                o, qq, 1.0 / D, m, ALU.mult, ALU.subtract), inc="VE")
            v_var = em.cnt["VE"]
            em.wait("scalar", "VE", v_var)
            em.op("scalar", lambda e, v_=var, r=rinv: e.activation(r, v_, AF.Ln, bias=LN_EPS))
            em.op("scalar", lambda e, r=rinv: e.activation(r, r, AF.Exp, scale=-0.5), inc="SE")
            s_rinv = em.cnt["SE"]
            em.wait("vector", "SE", s_rinv)
            em.op("vector", lambda e, m=mu, r=rinv, b=beta: e.tensor_scalar(
                b, m, r, -1.0, ALU.mult, ALU.mult), inc="VE")
            v_beta = em.cnt["VE"]
            em.wait("scalar", "VE", v_beta)
            em.op("scalar", lambda e, xs=x2_ap, xo=ng_ap, r=rinv, b=beta: e.activation(
                xo, xs, AF.Identity, scale=r, bias=b), inc="SE")
            s_xh = em.cnt["SE"]
            s_xhT[tt] = s_xh
            em.wait("tensor", "SE", s_xh)
            for b in (2, 3):
                gd = bguardC[b]
                em.wait("tensor", gd[0], gd[1])
            for fc in range(NFC):
                bank = 2 + fc // 4
                qq = fc % 4
                em.op("tensor", _tp(ps[bank][:, 128 * qq:128 * qq + 128],
                                    ng_ap[:, 128 * fc:128 * fc + 128], IDT,
                                    start=(qq == 0), stop=(qq == 3)),
                      inc="PE" if fc == NFC - 1 else None)
            pe_tp2 = em.cnt["PE"]
            pe_tp2T[tt] = pe_tp2
            em.wait("vector", "PE", pe_tp2)
            em.op("vector", lambda e, d=x2f_ap: e.tensor_copy(d[:, 0:512], ps[2][:, 0:512]))
            em.op("vector", lambda e, d=x2f_ap: e.tensor_copy(d[:, 512:1024], ps[3][:, 0:512]),
                  inc="VE")
            v_x2f = em.cnt["VE"]
            bguardC[2] = ("VE", v_x2f)
            bguardC[3] = ("VE", v_x2f)
            em.wait("tensor", "VE", v_x2f)
            gd = bguardC[6]
            em.wait("tensor", gd[0], gd[1])
            for kc in range(8):
                em.op("tensor", _mm(ps[6][:, 0:256], x2f_ap[:, 128 * kc:128 * kc + 128],
                                    ar[:, off["wpD"] + 256 * kc:off["wpD"] + 256 * kc + 256],
                                    kc == 0, kc == 7))
            for kc in range(8):
                em.op("tensor", _mm(ps[6][:, 256:260], x2f_ap[:, 128 * kc:128 * kc + 128],
                                    ar[:, off["simw"] + 4 * kc:off["simw"] + 4 * kc + 4],
                                    kc == 0, kc == 7),
                      inc="PE" if kc == 7 else None)
            pe_pl = em.cnt["PE"]
            gd = bguardC[7]
            em.wait("tensor", gd[0], gd[1])
            for kc in range(8):
                em.op("tensor", _mm(ps[7][0:15, 0:128],
                                    ar[:, off["wd"] + 15 * kc:off["wd"] + 15 * kc + 15],
                                    x2f_ap[:, 128 * kc:128 * kc + 128],
                                    kc == 0, kc == 7), inc="PE" if kc == 7 else None)
            pe_yw = em.cnt["PE"]
            em.wait("vector", "PE", pe_pl)
            proj_ap = ar[:, off["projt"]:off["projt"] + 256]
            em.op("vector", lambda e, d=proj_ap: e.tensor_tensor(d, ps[6][:, 0:256],
                                                                 A("bp"), ALU.add), inc="VE")
            v_proj = em.cnt["VE"]
            rC = ar[:, off["lnstC"] + 8:off["lnstC"] + 40]
            ssC = rC[:, 0:1]
            nm_ = rC[:, 1:2]
            sm_ = rC[:, 2:3]
            lg = rC[:, 4:8]
            exl = rC[:, 8:12]
            lw = rC[:, 12:16]
            msk = rC[:, 16:20]
            f_ = rC[:, 20:24]
            n012 = rC[:, 24:27]
            c01 = rC[:, 27:28]
            c012 = rC[:, 28:29]
            em.op("vector", lambda e, d=proj_ap, s=sqC: e.tensor_tensor(
                s[:, 0:256], d, d, ALU.mult))
            em.op("vector", lambda e, s=sqC, o=ssC: e.tensor_reduce(o, s[:, 0:256],
                                                                    AX.X, ALU.add))
            em.op("vector", lambda e, o=ssC: e.tensor_scalar(o, o, FFN_EPS2, None, ALU.max),
                  inc="VE")
            v_ssC = em.cnt["VE"]
            em.wait("scalar", "VE", v_ssC)
            em.op("scalar", lambda e, o=ssC: e.activation(o, o, AF.Ln))
            em.op("scalar", lambda e, o=ssC: e.activation(o, o, AF.Exp, scale=-0.5), inc="SE")
            s_rinvC = em.cnt["SE"]
            em.wait("vector", "SE", s_rinvC)
            em.op("vector", lambda e, o=lg: e.tensor_tensor(o, ps[6][:, 256:260], A("bpw"),
                                                            ALU.add), inc="VE")
            em.op("vector", lambda e, o=lg, s=ssC: e.tensor_scalar(
                o, o, s, ar[:, off["srep"]:off["srep"] + 1], ALU.mult, ALU.mult))
            em.op("vector", lambda e, o=lg, n=nm_: e.tensor_reduce(
                n, o, AX.X, ALU.max, negate=True), inc="VE")
            v_lg = em.cnt["VE"]
            em.wait("scalar", "VE", v_lg)
            em.op("scalar", lambda e, o=lg, x=exl, n=nm_: e.activation(
                x, o, AF.Exp, bias=n), inc="SE")
            s_ex = em.cnt["SE"]
            em.wait("vector", "SE", s_ex)
            em.op("vector", lambda e, x=exl, s=sm_: e.tensor_reduce(s, x, AX.X, ALU.add))
            em.op("vector", lambda e, s=sm_: e.reciprocal(s, s))
            em.op("vector", lambda e, x=exl, s=sm_, o=lw: e.tensor_scalar(
                o, x, s, None, ALU.mult))
            em.op("vector", lambda e, o=lw, n=nm_: e.tensor_reduce(
                n, o, AX.X, ALU.max, negate=True))
            em.op("vector", lambda e, o=lw, n=nm_, m=msk: e.tensor_scalar(
                m, o, n, 0.0, ALU.add, ALU.is_ge))
            em.op("vector", lambda e, m=msk, n=n012: e.tensor_scalar(
                n, m[:, 0:3], -1.0, 1.0, ALU.mult, ALU.add))
            em.op("vector", lambda e, n=n012, c=c01: e.tensor_tensor(
                c, n[:, 0:1], n[:, 1:2], ALU.mult))
            em.op("vector", lambda e, n=n012, c=c01, cc=c012: e.tensor_tensor(
                cc, c, n[:, 2:3], ALU.mult))
            em.op("vector", lambda e, m=msk, f=f_: e.tensor_copy(f[:, 0:1], m[:, 0:1]))
            em.op("vector", lambda e, m=msk, f=f_, n=n012: e.tensor_tensor(
                f[:, 1:2], n[:, 0:1], m[:, 1:2], ALU.mult))
            em.op("vector", lambda e, m=msk, f=f_, c=c01: e.tensor_tensor(
                f[:, 2:3], c, m[:, 2:3], ALU.mult))
            em.op("vector", lambda e, m=msk, f=f_, cc=c012: e.tensor_tensor(
                f[:, 3:4], cc, m[:, 3:4], ALU.mult))
            em.op("vector", lambda e, f=f_, o=lw: e.tensor_tensor(o, o, f, ALU.mult),
                  inc="VE")
            v_ret = em.cnt["VE"]
            rexp = ar[:, off["retf"]:off["retf"] + 15]
            em.op("vector", lambda e, o=lw, r=rexp: e.tensor_copy(r[:, 0:1], o[:, 0:1]))
            em.op("vector", lambda e, o=lw, r=rexp: e.tensor_copy(
                r[:, 1:3], o[:, 1:2].broadcast_to([128, 2])))
            em.op("vector", lambda e, o=lw, r=rexp: e.tensor_copy(
                r[:, 3:7], o[:, 2:3].broadcast_to([128, 4])))
            em.op("vector", lambda e, o=lw, r=rexp: e.tensor_copy(
                r[:, 7:15], o[:, 3:4].broadcast_to([128, 8])), inc="VE")
            v_rexpT[tt] = em.cnt["VE"]
            pe_ywT[tt] = pe_yw
            if tt >= 1:
                lateC_fin(tt - 1)
        if run_c:
            stageR(NT - 1)
            lateC_mm(NT - 1)
            lateC_fin(NT - 1)

        if not run_c:
            zb = ar[:, off["stgC"]:off["stgC"] + 1024]
            em.op("vector", lambda e, b=zb: e.memset(b, 0.0), inc="VE")
            vz = em.cnt["VE"]
            em.wait("sync", "VE", vz)
            for tt in range(NT):
                store(out_d[128 * tt:128 * tt + 128, :], zb, sem="STG2")
        wait_all_stores("sync")

        @block.tensor
        def _(e):
            for f in em.q["tensor"]:
                f(e)

        @block.vector
        def _(e):
            for f in em.q["vector"]:
                f(e)

        @block.scalar
        def _(e):
            for f in em.q["scalar"]:
                f(e)

        @block.gpsimd
        def _(e):
            for f in em.q["gpsimd"]:
                f(e)

        @block.sync
        def _(e):
            for f in em.q["sync"]:
                f(e)

    return nc


ng_store = {}

_NC_CACHE = {}


def _get_nc(debug=False):
    key = (bool(debug), PHASES, B_SC, B_STAGE)
    if key not in _NC_CACHE:
        ng_store.clear()
        _NC_CACHE[key] = build(debug)
    return _NC_CACHE[key]


def kernel(**inp):
    x = np.ascontiguousarray(np.asarray(inp["L0_attn_x"], dtype=np.float32))
    ins = {}
    for m, nm in zip("rwkvabg", ["W_r", "W_w", "W_k", "W_v", "W_a", "W_b", "W_g"]):
        ins[f"w_{m}T"] = np.ascontiguousarray(np.asarray(inp[f"L0_attn_{nm}"], np.float32).T)
    ins["wo_T"] = np.ascontiguousarray(np.asarray(inp["L0_attn_W_o"], np.float32).T)
    ins["ffnw_T"] = np.ascontiguousarray(np.asarray(inp["L0_ffn_W"], np.float32).T)
    ins["wp_T"] = np.ascontiguousarray(np.asarray(inp["L0_ffn_Wp"], np.float32).T)
    ins["wp_nat"] = np.ascontiguousarray(np.asarray(inp["L0_ffn_Wp"], np.float32))
    wd = np.concatenate([np.asarray(inp[f"L0_ffn_Wd{i}"], np.float32) for i in range(4)], 0)
    ins["wd_T"] = np.ascontiguousarray(wd.T)
    wu = np.concatenate([np.asarray(inp[f"L0_ffn_Wu{i}"], np.float32) for i in range(4)], 1)
    ins["wu_rho"] = np.ascontiguousarray(wu.T)
    ins["bo_rep"] = np.ascontiguousarray(
        np.broadcast_to(np.asarray(inp["L0_attn_b_o"], np.float32), (128, D)))
    ins["gnw_rep"] = np.ascontiguousarray(
        np.broadcast_to(np.asarray(inp["L0_attn_gn_w"], np.float32), (128, D)))
    ins["gnb_rep"] = np.ascontiguousarray(
        np.broadcast_to(np.asarray(inp["L0_attn_gn_b"], np.float32), (128, D)))
    ins["bp_rep"] = np.ascontiguousarray(
        np.broadcast_to(np.asarray(inp["L0_ffn_bp"], np.float32), (128, 256)))
    mixes = [np.asarray(inp[f"L0_attn_x_{m}"], np.float32).reshape(NFC, 128).T
             for m in "rwkvabg"]
    ins["mix_cat"] = np.ascontiguousarray(np.stack(mixes, 1).reshape(128, 56))
    ins["sim_in"] = np.ascontiguousarray(np.asarray(inp["L0_ffn_sim"], np.float32))
    ins["temp_in"] = np.ascontiguousarray(
        np.asarray(inp["L0_ffn_temp"], np.float32).reshape(1, 1))

    nc = _get_nc(DEBUG)
    in_maps = []
    for b in range(8):
        d = dict(ins)
        d["x_in"] = np.ascontiguousarray(x[b])
        in_maps.append(d)
    res = run_bass_kernel_spmd(nc, in_maps, core_ids=list(range(8)))
    out = np.stack([res.results[b]["out"] for b in range(8)], 0)
    if DEBUG:
        kernel._last = res
    return out
